# version 23
# baseline (speedup 1.0000x reference)
from contextlib import ExitStack
from dataclasses import dataclass

import numpy as np
import ml_dtypes
import concourse.bass as bass
import concourse.mybir as mybir
from concourse.bass_utils import run_bass_kernel_spmd

F32 = mybir.dt.float32
BF16 = mybir.dt.bfloat16
AF = mybir.ActivationFunctionType
ALU = mybir.AluOpType


@dataclass(frozen=True)
class Cfg:
    D: int = 4096
    T: int = 2048
    R: int = 128
    INT: int = 8192
    LAT: int = 2048
    NE: int = 8
    NPASS: int = 4
    eps: float = 1e-5

    @property
    def BLKS(self):
        return self.D // 128

    @property
    def TC(self):
        return self.T // self.NPASS

    @property
    def Tf(self):
        return self.TC // 2

    @property
    def TOKS(self):
        return self.T // 2

    @property
    def G4(self):
        return (self.D // 32) * 4


CFG = Cfg()
SEMS = ("ld0", "ld1", "xa0", "xa1", "xb0", "xb1", "rp0", "rp1", "bc", "xst",
        "yh0", "yh1", "xy0", "xy1", "xs0", "xs1", "xr0", "xr1",
        "hs0", "hs1", "hl0", "hl1", "ot0", "ot1", "T",
        "wl0", "wl1", "xy2", "xy3", "xs2", "xs3", "qv", "qs", "qg")
AUTOSYNC = {"v": "qv", "s": "qs", "g": "qg"}
KROW = 1024
DEBUG_DUMP = False
STRICT_SYNC = True


def nsplit(total, step=512):
    return [(i, min(step, total - i)) for i in range(0, total, step)]


class Sched:
    def __init__(self, nc, sems, cnt):
        self.nc, self.sems, self.cnt = nc, sems, cnt
        self.prog = []

    def op(self, eng, fn, waits=(), inc=None, dma=False):
        incs = []
        qn = AUTOSYNC.get(eng)
        if qn is not None and not dma:
            assert inc is None or inc == qn, (eng, inc)
            self.cnt[qn] += 1
            incs = [(qn, 1)]
        elif inc is not None:
            n = 16 if dma else 1
            self.cnt[inc] += n
            incs = [(inc, n)]
        self.prog.append((eng, tuple(waits), fn, incs, dma))

    def v(self, s):
        return self.cnt[s]

    def snap(self):
        return [(s, self.cnt[s]) for s in SEMS if self.cnt[s] > 0]

    def barrier(self):
        w = self.snap()
        for e in ("g", "v", "s", "t", "y"):
            self.prog.append((e, tuple(w), lambda en: en.nop(), [], True))

    def emit(self):
        with self.nc.Block() as block:
            def mk(key):
                def emitter(eng):
                    last = {}
                    qs = AUTOSYNC.get(key)
                    for (e, waits, fn, incs, is_dma) in self.prog:
                        if e != key:
                            continue
                        for (s, val) in waits:
                            if s == qs and not is_dma and not STRICT_SYNC:
                                continue
                            if val > 0 and last.get(s, 0) < val:
                                eng.wait_ge(self.sems[s], val)
                                last[s] = val
                        if STRICT_SYNC and qs is not None and not is_dma:
                            nq = self.cnt.get("em_" + qs, 0)
                            if nq > 0 and last.get(qs, 0) < nq:
                                eng.wait_ge(self.sems[qs], nq)
                                last[qs] = nq
                        instr = fn(eng)
                        for (s, n) in incs:
                            instr.then_inc(self.sems[s], n)
                            if s == qs:
                                self.cnt["em_" + qs] = self.cnt.get("em_" + qs, 0) + 1
                return emitter
            block.gpsimd(mk("g"))
            block.vector(mk("v"))
            block.scalar(mk("s"))
            block.tensor(mk("t"))
            block.sync(mk("y"))


def build_core_program(cfg: Cfg):
    nc = bass.Bass("TRN2", target_bir_lowering=False, debug=False)
    D, T, R, INT, LAT, NE = cfg.D, cfg.T, cfg.R, cfg.INT, cfg.LAT, cfg.NE
    BLKS, TC, Tf, TOKS, NP, G4 = cfg.BLKS, cfg.TC, cfg.Tf, cfg.TOKS, cfg.NPASS, cfg.G4
    RW = R + 32
    assert TC <= 512 and Tf * 2 == TC

    def dr(name, sh, dt=F32, kind=None):
        t = nc.dram_tensor(name, list(sh), dt, kind=kind) if kind else \
            nc.dram_tensor(name, list(sh), dt)
        return t.ap()

    x_dt = dr("x_dt", (D, T), kind="ExternalInput")
    xcols = dr("xcols", (D, TOKS), kind="ExternalInput")
    up_w = dr("up_w", (D, INT), kind="ExternalInput")
    down_w = dr("down_w", (INT, D), kind="ExternalInput")
    fc1_w = dr("fc1_w", (D, LAT), kind="ExternalInput")
    fc2_w = dr("fc2_w", (LAT, D), kind="ExternalInput")
    expert_w = dr("expert_w", (LAT, LAT), kind="ExternalInput")
    gate_w = dr("gate_w", (D, NE), kind="ExternalInput")
    xpw_t = dr("xpw_t", (D, RW), kind="ExternalInput")
    dtw_t = dr("dtw_t", (R, D), kind="ExternalInput")
    alog_il = dr("alog_il", (128, G4), kind="ExternalInput")
    b_cm = dr("b_cm", (128, BLKS), kind="ExternalInput")
    d_cm = dr("d_cm", (128, BLKS), kind="ExternalInput")
    gam_cm = dr("gam_cm", (128, BLKS), kind="ExternalInput")
    bet_cm = dr("bet_cm", (128, BLKS), kind="ExternalInput")
    r4_in = dr("r4", (128, 32), BF16, kind="ExternalInput")
    onesbf_in = dr("ones_bf", (128, 1), BF16, kind="ExternalInput")
    onesf_in = dr("ones_f", (1, 128), kind="ExternalInput")
    ident_in = dr("ident_f", (128, 128), kind="ExternalInput")
    mA_in = dr("mask_a", (128, 1), kind="ExternalInput")
    mB_in = dr("mask_b", (128, 1), kind="ExternalInput")
    out_dt = dr("out", (D, TOKS), kind="ExternalOutput")
    y_h = dr("y_h", (D, T))
    upb = dr("upb", (D, INT), BF16)
    downb = dr("downb", (INT, D), BF16)
    fc1b = dr("fc1b", (D, LAT), BF16)
    fc2b = dr("fc2b", (LAT, D), BF16)
    expb = dr("expb", (LAT, LAT), BF16)
    x1_h = dr("x1_h", (D, TOKS))
    h_h = dr("h_h", (INT, Tf), BF16)
    xpB_h = dr("xpB_h", (32, TC), BF16)

    with ExitStack() as big:
        sb = lambda n, sh, dt=F32: big.enter_context(nc.sbuf_tensor(n, list(sh), dt))
        sems, cnt = {}, {}
        for s in SEMS:
            sems[s] = big.enter_context(nc.semaphore(s))
            cnt[s] = 0
        ps = [big.enter_context(nc.psum_tensor(f"ps{i}", [128, 512], F32))
              for i in range(8)]

        gate_sb = sb("gate_sb", (128, BLKS * NE), BF16)
        gam_sb = sb("gam_sb", (128, BLKS))
        bet_sb = sb("bet_sb", (128, BLKS))
        r4_sb = sb("r4_sb", (128, 32), BF16)
        onesbf = sb("onesbf", (128, 1), BF16)
        onesf = sb("onesf", (1, 128))
        ident = sb("ident", (128, 128))
        mA_sb = sb("mA_sb", (128, 1))
        mB_sb = sb("mB_sb", (128, 1))
        epscol = sb("epscol", (128, 1))
        xpw_sb = sb("xpw_sb", (128, BLKS * RW), BF16)
        dtw_sb = sb("dtw_sb", (R, D), BF16)
        A_il = sb("A_il", (128, G4))
        bcol = sb("bcol", (128, BLKS))
        dcol = sb("dcol", (128, BLKS))
        rbc = sb("rbc", (128, T))
        mrbc = sb("mrbc", (128, T))
        states = sb("states", (128, G4))

        S0 = Sched(nc, sems, cnt)
        with ExitStack() as p0:
            p0sb = lambda n, sh, dt=F32: p0.enter_context(
                nc.sbuf_tensor(n, list(sh), dt))
            xst = [p0sb(f"xst{i}", (128, T)) for i in range(2)]
            xbf = [p0sb(f"xbf{i}", (128, T), BF16) for i in range(2)]
            xsq = [p0sb(f"xsq{i}", (128, T), BF16) for i in range(2)]
            atmp = p0sb("atmp", (128, G4))
            rows4 = p0sb("rows4", (1, 4 * T))
            mrow = rows4.ap()[:, 0:T]
            e2row = rows4.ap()[:, T:2 * T]
            rrow = rows4.ap()[:, 2 * T:3 * T]
            trow = rows4.ap()[:, 3 * T:4 * T]

            for dst, src in [
                (xpw_sb, xpw_t.rearrange("(b p) r -> p b r", p=128)),
                (dtw_sb, dtw_t),
                (gate_sb, gate_w.rearrange("(b p) e -> p b e", p=128)),
                (A_il, alog_il), (bcol, b_cm), (dcol, d_cm),
                (gam_sb, gam_cm), (bet_sb, bet_cm), (r4_sb, r4_in),
                (onesbf, onesbf_in), (onesf, onesf_in), (ident, ident_in),
                (mA_sb, mA_in), (mB_sb, mB_in),
            ]:
                S0.op("g", (lambda d=dst, s_=src: (lambda e: e.dma_start(
                    out=d[:, :], in_=s_)))(), inc="ld0", dma=True)
            ld0 = S0.v("ld0")
            S0.op("s", lambda e: e.activation(atmp[:, :], A_il[:, :], AF.Exp),
                  waits=[("ld0", ld0)], inc="qs")
            S0.op("v", lambda e: e.tensor_scalar(
                A_il[:, :], atmp[:, :], -1.0, None, ALU.mult),
                waits=[("qs", S0.v("qs"))], inc="qv")
            S0.op("v", lambda e: e.memset(states[:, :], 0.0), inc="qv")
            S0.op("v", lambda e: e.memset(epscol[:, :], cfg.eps), inc="qv")

            NCHT = nsplit(T)
            assert len(NCHT) <= 4
            sfree = {}
            for blk in range(BLKS):
                p = blk % 2
                sx = f"xa{p}"
                w = [("ld0", ld0)] if blk < 2 else [("T", sfree[blk - 2])]
                S0.op("y", (lambda p=p, b=blk: (lambda e: e.dma_start(
                    out=xst[p][:, :], in_=x_dt[b * 128:(b + 1) * 128, :])))(),
                    waits=w, inc=sx, dma=True)
                S0.op("s", (lambda p=p: (lambda e: e.copy(
                    xbf[p][:, :], xst[p][:, :])))(),
                    waits=[(sx, S0.v(sx))], inc=None)
                S0.op("s", (lambda p=p: (lambda e: e.square(
                    xsq[p][:, :], xbf[p][:, :])))(), inc="qs")
                sw = [("qs", S0.v("qs"))]
                for i, (o, n) in enumerate(NCHT):
                    S0.op("t", (lambda p=p, i=i, o=o, n=n, b=blk: (lambda e: e.matmul(
                        ps[i][0:1, 0:n], onesbf[:, :], xbf[p][:, o:o + n],
                        start=(b == 0), stop=(b == BLKS - 1))))(),
                        waits=sw if i == 0 else (), inc=None)
                    S0.op("t", (lambda p=p, i=i, o=o, n=n, b=blk: (lambda e: e.matmul(
                        ps[4 + i][0:1, 0:n], onesbf[:, :], xsq[p][:, o:o + n],
                        start=(b == 0), stop=(b == BLKS - 1))))(),
                        inc="T" if i == len(NCHT) - 1 else None)
                sfree[blk] = S0.v("T")
            tw = [("T", S0.v("T"))]
            for i, (o, n) in enumerate(NCHT):
                S0.op("s", (lambda i=i, o=o, n=n: (lambda e: e.activation(
                    mrow[0:1, o:o + n], ps[i][0:1, 0:n], AF.Copy, scale=1.0 / D)))(),
                    waits=tw if i == 0 else (), inc=None)
                S0.op("s", (lambda i=i, o=o, n=n: (lambda e: e.activation(
                    e2row[0:1, o:o + n], ps[4 + i][0:1, 0:n], AF.Copy, scale=1.0 / D)))(),
                    inc="qs" if i == len(NCHT) - 1 else None)
            S0.op("v", lambda e: e.tensor_tensor(
                trow[:, :], mrow[:, :], mrow[:, :], ALU.mult),
                waits=[("qs", S0.v("qs"))], inc=None)
            S0.op("v", lambda e: e.tensor_tensor(
                trow[:, :], e2row[:, :], trow[:, :], ALU.subtract), inc="qv")
            S0.op("s", lambda e: e.activation(
                rrow[:, :], trow[:, :], AF.Ln, bias=epscol[0:1, 0:1]),
                waits=[("qv", S0.v("qv"))], inc=None)
            S0.op("s", lambda e: e.activation(
                rrow[:, :], rrow[:, :], AF.Exp, scale=-0.5), inc="qs")
            S0.op("v", lambda e: e.tensor_tensor(
                trow[:, :], mrow[:, :], rrow[:, :], ALU.mult),
                waits=[("qs", S0.v("qs"))], inc="qv")
            bw = [("qv", S0.v("qv"))]
            for i, (o, n) in enumerate(NCHT):
                S0.op("t", (lambda o=o, n=n: (lambda e: e.matmul(
                    ps[0][:, 0:n], onesf[:, :], rrow[0:1, o:o + n],
                    start=True, stop=True)))(), waits=bw, inc="T")
                S0.op("v", (lambda o=o, n=n: (lambda e: e.tensor_copy(
                    rbc[:, o:o + n], ps[0][:, 0:n])))(),
                    waits=[("T", S0.v("T"))], inc="qv")
                S0.op("t", (lambda o=o, n=n: (lambda e: e.matmul(
                    ps[1][:, 0:n], onesf[:, :], trow[0:1, o:o + n],
                    start=True, stop=True)))(),
                    waits=[("qv", S0.v("qv"))], inc="T")
                S0.op("v", (lambda o=o, n=n: (lambda e: e.tensor_copy(
                    mrbc[:, o:o + n], ps[1][:, 0:n])))(),
                    waits=[("T", S0.v("T"))], inc="qv")
                bw = [("qv", S0.v("qv"))]
            S0.barrier()
            S0.emit()

        S = Sched(nc, sems, cnt)
        with ExitStack() as mg_:
            m_ = lambda n, sh, dt=F32: mg_.enter_context(nc.sbuf_tensor(n, list(sh), dt))
            xs1 = [m_(f"xs1_{i}", (128, TC)) for i in range(2)]
            xs2 = [m_(f"xs2_{i}", (128, TC)) for i in range(2)]
            tmpa = m_("tmpa", (128, TC))
            tmpb = tmpa
            xlnbf = [m_(f"xlnbf{i}", (128, TC), BF16) for i in range(2)]
            dtin = m_("dtin", (R, TC), BF16)
            xpB = m_("xpB", (32, TC), BF16)
            es = m_("es", (128, TC))
            dtb = [m_(f"dtb{i}", (128, TC)) for i in range(2)]
            xln2 = [m_(f"xln2_{i}", (128, TC)) for i in range(2)]
            ub = [m_(f"ub{i}", (128, TC), BF16) for i in range(2)]
            dtrep = [m_(f"dtrep{i}", (128, TC)) for i in range(2)]
            urep = [m_(f"urep{i}", (128, TC), BF16) for i in range(2)]
            BrepB = [m_(f"Brep{i}", (128, TC), BF16) for i in range(4)]
            CrepB = [m_(f"Crep{i}", (128, TC), BF16) for i in range(4)]
            NDA = 5
            dAb = [m_(f"dA{i}", (128, TC)) for i in range(NDA)]
            dBxb = [m_(f"dBx{i}", (128, TC), BF16) for i in range(NDA)]
            HSD = 7
            h_sb = [m_(f"h_sb{i}", (128, TC), BF16) for i in range(HSD)]
            NHC = 7
            hCb = [m_(f"hC{i}", (128, TC), BF16) for i in range(NHC)]
            yb0 = m_("yb0", (128, TC))
            yb = [yb0, yb0]
            arenaA = m_("arenaA", (128, BLKS * Tf))
            xlnff = arenaA.ap().bitcast(BF16)
            s_sb = arenaA
            lat_sb = arenaA.ap().bitcast(BF16)[:, BLKS * Tf:BLKS * Tf + (LAT // 128) * Tf]
            moe_sb = m_("moe_sb", (128, (LAT // 128) * Tf), BF16)
            NKW = KROW // 128
            wb = [m_(f"wb{i}", (128, NKW * 512), BF16) for i in range(2)]
            hio = [m_(f"hio{i}", (128, NKW * Tf), BF16) for i in range(2)]
            xa = [m_(f"xa{i}", (128, Tf)) for i in range(3)]
            yba = [m_(f"yba{i}", (128, Tf)) for i in range(3)]
            ybb = [m_(f"ybb{i}", (128, Tf)) for i in range(3)]
            x1b = [m_(f"x1b{i}", (128, Tf)) for i in range(3)]
            x1r = [m_(f"x1r{i}", (128, Tf)) for i in range(2)]
            ob = [m_(f"ob{i}", (128, Tf)) for i in range(2)]
            ft1 = m_("ft1", (128, Tf))
            ft2 = m_("ft2", (128, Tf))
            ftst = m_("ftst", (128, Tf))
            fu = [m_(f"fu{i}", (128, Tf)) for i in range(2)]
            bca = m_("bca", (128, Tf))
            bcb = m_("bcb", (128, Tf))
            sbf = [m_(f"sbf{i}", (128, Tf), BF16) for i in range(3)]
            ssq = [m_(f"ssq{i}", (128, Tf), BF16) for i in range(3)]
            sbf3 = [m_(f"sbf3_{i}", (128, Tf), BF16) for i in range(2)]
            ssq3 = [m_(f"ssq3_{i}", (128, Tf), BF16) for i in range(2)]
            MCH = nsplit(Tf, 128)
            rwcols = m_("rwcols", (128, len(MCH)))
            mx8 = m_("mx8", (128, 8))
            rwbc = m_("rwbc", (128, Tf), BF16)
            m2r = m_("m2r", (1, Tf))
            e2r = m_("e2r", (1, Tf))
            r2r = e2r
            t2r = m_("t2r", (1, Tf))
            rwr1 = t2r
            assert Tf <= 256 and len(MCH) <= 2

            xa_free = {}
            yps_free = {}
            ybl_free = {}
            xln_free = {}
            xp_mm = {}
            dtps_free = {}
            rep_free = {}
            dtb_cons = {}
            cons = {}
            pend_stc = {}
            st = {"xp_evac": 0, "bc_done": 0}
            psdt = [ps[0], ps[1]]
            psy = [ps[2], ps[3]]
            wbi = [0]
            ftrot = [0]
            upst = [0]
            fu_free = {}
            wb_free = {}
            hio_free = {}
            ws_ps_free = {}
            x1_done = {}
            x1c = {}
            odone = {}
            stat_done = {}
            ln2_read = {}
            h_rd = {}
            MGW = 384
            NMG_UP = len(nsplit(INT, MGW))

            pending = []

            def pump(n):
                done = 0
                while pending and done < n:
                    try:
                        next(pending[0])
                        done += 1
                    except StopIteration:
                        pending.pop(0)

            def sched_precast():
                first = [("xa0", S.v("xa0")), ("xa1", S.v("xa1"))]
                for wt_d, wt_s, krows in [(fc1b, fc1_w, D), (expb, expert_w, LAT),
                                          (upb, up_w, D), (downb, down_w, INT),
                                          (fc2b, fc2_w, LAT)]:
                    for (ro, rn) in nsplit(krows, 1024):
                        S.op("g", (lambda d=wt_d, s_=wt_s, ro=ro, rn=rn: (lambda e: e.dma_start(
                            out=d[ro:ro + rn, :], in_=s_[ro:ro + rn, :])))(),
                            waits=first, inc="ld0", dma=True)
                        first = []
                st["precast"] = S.v("ld0")

            def p1_subA(c):
                co = c * TC
                for blk in range(BLKS):
                    p = blk % 2
                    sx = f"xa{p}"
                    if (c, blk - 2) in xa_free:
                        w = [("qs", xa_free[(c, blk - 2)])]
                    elif c > 0:
                        w = [("qs", xa_free[(c - 1, BLKS - 2 + blk)])]
                    else:
                        w = []
                    S.op("y", (lambda p=p, b=blk, co=co: (lambda e: e.dma_start(
                        out=xs1[p][:, :],
                        in_=x_dt[b * 128:(b + 1) * 128, co:co + TC])))(),
                        waits=w, inc=sx, dma=True)
                    S.op("v", (lambda p=p, co=co: (lambda e: e.tensor_tensor(
                        tmpa[:, :], xs1[p][:, :], rbc[:, co:co + TC], ALU.mult)))(),
                        waits=[(sx, S.v(sx))], inc=None)
                    S.op("v", (lambda p=p, co=co: (lambda e: e.tensor_tensor(
                        xs1[p][:, :], tmpa[:, :], mrbc[:, co:co + TC], ALU.subtract)))(),
                        inc="qv")
                    cw = [("qv", S.v("qv"))]
                    if (c, blk - 2) in xp_mm:
                        cw.append(("T", xp_mm[(c, blk - 2)]))
                    elif c > 0:
                        cw.append(("T", xp_mm[(c - 1, BLKS - 2 + blk)]))
                    S.op("s", (lambda p=p: (lambda e: e.copy(
                        xlnbf[p][:, :], xs1[p][:, :])))(),
                        waits=cw, inc="qs")
                    xa_free[(c, blk)] = S.v("qs")
                    sw = [("qs", S.v("qs"))]
                    if blk == 0:
                        sw += [("qg", S.v("qg")), ("qv", S.v("qv"))]
                    S.op("t", (lambda p=p, b=blk: (lambda e: e.matmul(
                        ps[0][:, 0:TC], xpw_sb[:, b * RW:b * RW + 128],
                        xlnbf[p][:, :],
                        start=(b == 0), stop=(b == BLKS - 1))))(),
                        waits=sw, inc=None)
                    S.op("t", (lambda p=p, b=blk: (lambda e: e.matmul(
                        ps[1][0:32, 0:TC],
                        xpw_sb[:, b * RW + 128:(b + 1) * RW],
                        xlnbf[p][:, :],
                        start=(b == 0), stop=(b == BLKS - 1))))(),
                        inc="T")
                    xp_mm[(c, blk)] = S.v("T")
                    pump(1)
                tw = [("T", S.v("T"))]
                S.op("v", lambda e: e.tensor_copy(dtin[:, :], ps[0][0:R, 0:TC]),
                     waits=tw, inc=None)
                S.op("v", lambda e: e.tensor_copy(xpB[:, :], ps[1][0:32, 0:TC]),
                     inc="qv")
                st["xp_evac"] = S.v("qv")
                w = [("qv", st["xp_evac"])]
                if c > 0:
                    w.append(("bc", st["bc_done"]))
                S.op("y", lambda e: e.dma_start(out=xpB_h, in_=xpB[:, :]),
                     waits=w, inc="xst", dma=True)
                xpB_st = S.v("xst")
                for q in range(4):
                    S.op("s", (lambda q=q: (lambda e: e.dma_start(
                        out=BrepB[q][:, :],
                        in_=xpB_h[4 * q:4 * q + 4, :].unsqueeze(0).broadcast_to([32, 4, TC]))))(),
                        waits=[("xst", xpB_st)], inc="bc", dma=True)
                    S.op("s", (lambda q=q: (lambda e: e.dma_start(
                        out=CrepB[q][:, :],
                        in_=xpB_h[16 + 4 * q:16 + 4 * q + 4, :].unsqueeze(0).broadcast_to([32, 4, TC]))))(),
                        inc="bc", dma=True)
                st["bc_done"] = S.v("bc")

            def p1_subB_blk(c, blk):
                co = c * TC
                par = blk % 2
                bc_done = st["bc_done"]
                w = [("qv", st["xp_evac"])]
                if (c, blk - 2) in dtps_free:
                    w.append(("qs", dtps_free[(c, blk - 2)]))
                elif c > 0:
                    w.append(("qs", dtps_free[(c - 1, BLKS - 2 + blk)]))
                S.op("t", (lambda b=blk, pr=par: (lambda e: e.matmul(
                    psdt[pr][:, 0:TC], dtw_sb[:, b * 128:(b + 1) * 128],
                    dtin[:, :], start=True, stop=True)))(),
                    waits=w, inc="T")
                sw = [("T", S.v("T"))]
                if (c, blk - 2) in dtb_cons:
                    r0, r1 = dtb_cons[(c, blk - 2)]
                    sw += [("rp0", r0), ("rp1", r1)]
                elif c > 0:
                    r0, r1 = dtb_cons[(c - 1, BLKS - 2 + blk)]
                    sw += [("rp0", r0), ("rp1", r1)]
                S.op("s", (lambda b=blk, pr=par: (lambda e: e.activation(
                    es[:, :], psdt[pr][:, 0:TC], AF.Exp,
                    bias=bcol[:, b:b + 1])))(),
                    waits=sw, inc=None)
                S.op("s", (lambda pr=par: (lambda e: e.activation(
                    dtb[pr][:, :], es[:, :], AF.Ln, bias=1.0)))(), inc="qs")
                dtps_free[(c, blk)] = S.v("qs")
                dt_done = S.v("qs")
                sxb = f"xb{par}"
                if (c, blk - 2) in xln_free:
                    w = [("qv", xln_free[(c, blk - 2)])]
                elif c > 0:
                    w = [("qv", xln_free[(c - 1, BLKS - 2 + blk)])]
                else:
                    w = []
                S.op("y", (lambda pr=par, b=blk, co=co: (lambda e: e.dma_start(
                    out=xs2[pr][:, :],
                    in_=x_dt[b * 128:(b + 1) * 128, co:co + TC])))(),
                    waits=w, inc=sxb, dma=True)
                S.op("v", (lambda pr=par, co=co: (lambda e: e.tensor_tensor(
                    tmpb[:, :], xs2[pr][:, :], rbc[:, co:co + TC], ALU.mult)))(),
                    waits=[(sxb, S.v(sxb))], inc=None)
                S.op("v", (lambda pr=par, co=co: (lambda e: e.tensor_tensor(
                    xln2[pr][:, :], tmpb[:, :], mrbc[:, co:co + TC], ALU.subtract)))(),
                    inc=None)
                S.op("v", (lambda pr=par: (lambda e: e.tensor_tensor(
                    ub[pr][:, :], dtb[pr][:, :], xln2[pr][:, :], ALU.mult)))(),
                    waits=[("qs", dt_done)], inc="qv")
                u_done = S.v("qv")
                for gl in range(4):
                    gp = gl % 2
                    rp = f"rp{gp}"
                    gslot = c * BLKS * 4 + blk * 4 + gl
                    w = [("qv", u_done), ("bc", bc_done)]
                    if gslot - 2 in rep_free:
                        rs, rv = rep_free[gslot - 2]
                        w += [("qs", rs), ("qv", rv)]
                    S.op("s", (lambda pr=par, g=gp, gl=gl: (lambda e: e.dma_start(
                        out=dtrep[g][:, :],
                        in_=dtb[pr].ap()[32 * gl:32 * gl + 32, :].unsqueeze(1).broadcast_to([32, 4, TC]))))(),
                        waits=w, inc=rp, dma=True)
                    S.op("s", (lambda pr=par, g=gp, gl=gl: (lambda e: e.dma_start(
                        out=urep[g][:, :],
                        in_=ub[pr].ap()[32 * gl:32 * gl + 32, :].unsqueeze(1).broadcast_to([32, 4, TC]))))(),
                        inc=rp, dma=True)
                    rep_rdy = S.v(rp)
                    dtb_cons[(c, blk)] = (S.v("rp0"), S.v("rp1"))
                    lastS = lastV = 0
                    for q in range(4):
                        tile = (blk * 4 + gl) * 4 + q
                        gtile = c * G4 + tile
                        bp = gtile % NDA
                        hsi = gtile % HSD
                        w = [(rp, rep_rdy)]
                        if ("dA", gtile - NDA) in cons:
                            w.append(("qv", cons[("dA", gtile - NDA)]))
                        S.op("s", (lambda g=gp, b=bp, col=tile: (lambda e: e.activation(
                            dAb[b][:, :], dtrep[g][:, :], AF.Exp,
                            scale=A_il[:, col:col + 1])))(),
                            waits=w, inc="qs")
                        lastS = S.v("qs")
                        if gtile - NDA in pend_stc:
                            t2, h2, v2 = pend_stc.pop(gtile - NDA)
                            S.op("s", (lambda t=t2, h=h2: (lambda e: e.copy(
                                states[:, t:t + 1], h_sb[h][:, TC - 1:TC])))(),
                                waits=[("qv", v2)], inc="qs")
                            lastS = S.v("qs")
                        w = [(rp, rep_rdy), ("bc", bc_done)]
                        S.op("v", (lambda g=gp, b=bp, q=q: (lambda e: e.tensor_tensor(
                            dBxb[b][:, :], urep[g][:, :], BrepB[q][:, :], ALU.mult)))(),
                            waits=w, inc="qv")
                        lastV = S.v("qv")
                        tw = [("qs", lastS)]
                        S.op("v", (lambda b=bp, h=hsi, t=tile: (lambda e: e.tensor_tensor_scan(
                            h_sb[h][:, :], dAb[b][:, :], dBxb[b][:, :],
                            states[:, t:t + 1], ALU.mult, ALU.add)))(),
                            waits=tw, inc="qv")
                        cons[("dA", gtile)] = S.v("qv")
                        if c < NP - 1:
                            pend_stc[gtile] = (tile, hsi, S.v("qv"))
                        bph = gtile % NHC
                        w = []
                        if ("hC", gtile - NHC) in cons:
                            w.append(("T", cons[("hC", gtile - NHC)]))
                        S.op("v", (lambda b=bph, h=hsi, q=q: (lambda e: e.tensor_tensor(
                            hCb[b][:, :], h_sb[h][:, :], CrepB[q][:, :], ALU.mult)))(),
                            waits=w, inc="qv")
                        vhc = S.v("qv")
                        w = [("qv", vhc)]
                        if q == 0 and gl == 0:
                            if (c, blk - 2) in yps_free:
                                w.append(("qv", yps_free[(c, blk - 2)]))
                            elif c > 0:
                                w.append(("qv", yps_free[(c - 1, BLKS - 2 + blk)]))
                        S.op("t", (lambda b=bph, gl=gl, q=q, pr=par: (lambda e: e.matmul(
                            psy[pr][32 * gl:32 * gl + 32, 0:TC], r4_sb[:, :],
                            hCb[b][:, :],
                            start=(q == 0), stop=(q == 3),
                            tile_position=(0, 32 * gl))))(),
                            waits=w, inc="T")
                        cons[("hC", gtile)] = S.v("T")
                    rep_free[gslot] = (lastS, lastV)
                    pump(2)
                syh = f"yh{par}"
                pyh = f"yh{(blk - 1) % 2}"
                w = [("T", S.v("T"))]
                if (c, blk - 1) in ybl_free:
                    w.append((pyh, ybl_free[(c, blk - 1)]))
                elif c > 0:
                    w.append((pyh, ybl_free[(c - 1, BLKS - 1)]))
                S.op("v", (lambda pr=par, b=blk: (lambda e: e.scalar_tensor_tensor(
                    yb[pr][:, :], xln2[pr][:, :],
                    dcol[:, b:b + 1], psy[pr][:, 0:TC],
                    ALU.mult, ALU.add)))(),
                    waits=w, inc="qv")
                xln_free[(c, blk)] = S.v("qv")
                yps_free[(c, blk)] = S.v("qv")
                S.op("y", (lambda pr=par, b=blk, co=co: (lambda e: e.dma_start(
                    out=y_h[b * 128:(b + 1) * 128, co:co + TC],
                    in_=yb[pr][:, :])))(),
                    waits=[("qv", S.v("qv"))], inc=syh, dma=True)
                ybl_free[(c, blk)] = S.v(syh)

            def p2_stats_blk(pss, blk):
                pc = pss * Tf
                yA = pss * TC
                yB = pss * TC + Tf
                hc = (pss % 2) * Tf
                par = blk % 3
                sxy, sxs = f"xy{par}", f"xs{par}"
                w = [(sxs, x1_done[par])] if par in x1_done else []
                w.append((f"yh{blk % 2}", ybl_free[(pss, blk)]))
                S.op("y", (lambda p=par, b=blk, pc=pc: (lambda e: e.dma_start(
                    out=xa[p][:, :], in_=xcols[b * 128:(b + 1) * 128, pc:pc + Tf])))(),
                    waits=w, inc=sxy, dma=True)
                S.op("y", (lambda p=par, b=blk, yA=yA: (lambda e: e.dma_start(
                    out=yba[p][:, :], in_=y_h[b * 128:(b + 1) * 128, yA:yA + Tf])))(),
                    inc=sxy, dma=True)
                S.op("y", (lambda p=par, b=blk, yB=yB: (lambda e: e.dma_start(
                    out=ybb[p][:, :], in_=y_h[b * 128:(b + 1) * 128, yB:yB + Tf])))(),
                    inc=sxy, dma=True)
                stw = [(sxy, S.v(sxy))]
                if par in x1c:
                    stw.append(("qs", x1c[par]))
                S.op("v", (lambda p=par: (lambda e: e.scalar_tensor_tensor(
                    ftst[:, :], yba[p][:, :], mA_sb[:, 0:1], xa[p][:, :],
                    ALU.mult, ALU.add)))(),
                    waits=stw, inc=None)
                S.op("v", (lambda p=par: (lambda e: e.scalar_tensor_tensor(
                    x1b[p][:, :], ybb[p][:, :], mB_sb[:, 0:1], ftst[:, :],
                    ALU.mult, ALU.add)))(), inc="qv")
                S.op("y", (lambda p=par, b=blk, pc=pc: (lambda e: e.dma_start(
                    out=x1_h[b * 128:(b + 1) * 128, pc:pc + Tf], in_=x1b[p][:, :])))(),
                    waits=[("qv", S.v("qv"))], inc=sxs, dma=True)
                x1_done[par] = S.v(sxs)
                w = [("qv", S.v("qv"))]
                if par in stat_done:
                    w.append(("T", stat_done[par]))
                S.op("s", (lambda p=par: (lambda e: e.copy(
                    sbf[p][:, :], x1b[p][:, :])))(), waits=w, inc=None)
                S.op("s", (lambda p=par: (lambda e: e.square(
                    ssq[p][:, :], sbf[p][:, :])))(), inc="qs")
                x1c[par] = S.v("qs")

            def p2_stats_mm(pss, blk):
                hc = (pss % 2) * Tf
                par = blk % 3
                sw = [("qs", x1c[par])]
                if blk == 0 and pss >= 1:
                    sw.append(("qs", ln2_read[pss - 1]))
                S.op("t", (lambda p=par, b=blk, hc=hc: (lambda e: e.matmul(
                    ps[7][0:1, hc:hc + Tf], onesbf[:, :], sbf[p][:, :],
                    start=(b == 0), stop=(b == BLKS - 1),
                    tile_position=(0, 0))))(),
                    waits=sw, inc=None)
                S.op("t", (lambda p=par, b=blk, hc=hc: (lambda e: e.matmul(
                    ps[7][32:33, hc:hc + Tf], onesbf[:, :], ssq[p][:, :],
                    start=(b == 0), stop=(b == BLKS - 1),
                    tile_position=(0, 32))))(), inc="T")
                stat_done[par] = S.v("T")

            def rows_chain(src_m, src_q, waits):
                S.op("s", (lambda sm=src_m: (lambda e: e.activation(
                    m2r[0:1, 0:Tf], sm, AF.Copy, scale=1.0 / D)))(),
                    waits=waits, inc=None)
                S.op("s", (lambda sq=src_q: (lambda e: e.activation(
                    e2r[0:1, 0:Tf], sq, AF.Copy, scale=1.0 / D)))(),
                    inc="qs")
                S.op("v", lambda e: e.tensor_tensor(
                    t2r[:, :], m2r[:, :], m2r[:, :], ALU.mult),
                    waits=[("qs", S.v("qs"))], inc=None)
                S.op("v", lambda e: e.tensor_tensor(
                    t2r[:, :], e2r[:, :], t2r[:, :], ALU.subtract), inc="qv")
                S.op("s", lambda e: e.activation(
                    r2r[:, :], t2r[:, :], AF.Ln, bias=epscol[0:1, 0:1]),
                    waits=[("qv", S.v("qv"))], inc=None)
                S.op("s", lambda e: e.activation(
                    r2r[:, :], r2r[:, :], AF.Exp, scale=-0.5), inc="qs")
                S.op("v", lambda e: e.tensor_tensor(
                    t2r[:, :], m2r[:, :], r2r[:, :], ALU.mult),
                    waits=[("qs", S.v("qs"))], inc="qv")

            def bc_pair():
                S.op("t", lambda e: e.matmul(
                    ps[6][:, 0:Tf], onesf[:, :], r2r[0:1, 0:Tf], start=True, stop=True),
                    waits=[("qv", S.v("qv"))], inc="T")
                S.op("v", lambda e: e.tensor_copy(bca[:, :], ps[6][:, 0:Tf]),
                     waits=[("T", S.v("T"))], inc="qv")
                S.op("t", lambda e: e.matmul(
                    ps[6][:, 0:Tf], onesf[:, :], t2r[0:1, 0:Tf], start=True, stop=True),
                    waits=[("qv", S.v("qv"))], inc="T")
                S.op("v", lambda e: e.tensor_copy(bcb[:, :], ps[6][:, 0:Tf]),
                     waits=[("T", S.v("T"))], inc="qv")

            def ws_gen(wd, Krows, OD, rhs_of_k, evac, extra_w=()):
                for mg, (mgo, mgn) in enumerate(nsplit(OD, MGW)):
                    nml = (mgn + 127) // 128
                    assert mgn % 128 == 0 and nml <= 3
                    rch = nsplit(Krows, KROW)
                    for ri, (ro, rn) in enumerate(rch):
                        wi = wbi[0] % 2
                        wbi[0] += 1
                        sld = f"wl{wi}"
                        w = list(extra_w) + [("ld0", st["precast"])]
                        if wi in wb_free:
                            w.append(("T", wb_free[wi]))
                        S.op("y", (lambda wd=wd, ro=ro, rn=rn, mgo=mgo, mgn=mgn, wi=wi: (lambda e: e.dma_start(
                            out=wb[wi][:, 0:(rn // 128) * mgn],
                            in_=wd[ro:ro + rn, mgo:mgo + mgn].rearrange("(k p) c -> p k c", p=128))))(),
                            waits=w, inc=sld, dma=True)
                        lw = [(sld, S.v(sld))]
                        if ri == 0 and mg >= 1:
                            lw.append(("qs", ws_ps_free.get("S", 0)))
                            lw.append(("qv", ws_ps_free.get("V", 0)))
                        for k in range(rn // 128):
                            for ml in range(nml):
                                S.op("t", (lambda wi=wi, k=k, ml=ml, mgn=mgn, ro=ro,
                                           pt=ps[4 + ml], rhs=rhs_of_k,
                                           first=(ri == 0 and k == 0),
                                           last=(ri == len(rch) - 1 and k == rn // 128 - 1): (lambda e: e.matmul(
                                    pt[0:128, 0:Tf],
                                    wb[wi][:, k * mgn + ml * 128:k * mgn + ml * 128 + 128],
                                    rhs(ro // 128 + k), start=first, stop=last)))(),
                                    waits=lw if (k == 0 and ml == 0) else (),
                                    inc="T" if (k == rn // 128 - 1 and ml == nml - 1) else None)
                        wb_free[wi] = S.v("T")
                        yield
                    evac(mg, mgo, mgn, nml, [("T", S.v("T"))])
                    yield

            def p2_pass_gen(pss, x1d):
                pc = pss * Tf
                hc = (pss % 2) * Tf
                rows_chain(ps[7][0:1, hc:hc + Tf], ps[7][32:33, hc:hc + Tf],
                           [("T", S.v("T"))])
                ln2_read[pss] = S.v("qs")
                for _ in range(5):
                    yield
                bc_pair()
                yield
                xe = {}
                for blk in range(BLKS):
                    par = blk % 2
                    sxr = f"xr{par}"
                    w = ([("qv", S.v("qv"))] +
                         [(f"xs{j}", x1d[j]) for j in sorted(x1d)]
                         if blk < 2 else [("qv", xe[blk - 2])])
                    S.op("y", (lambda p=par, b=blk, pc=pc: (lambda e: e.dma_start(
                        out=x1r[p][:, :], in_=x1_h[b * 128:(b + 1) * 128, pc:pc + Tf])))(),
                        waits=w, inc=sxr, dma=True)
                    S.op("v", (lambda p=par: (lambda e: e.tensor_tensor(
                        ft1[:, :], x1r[p][:, :], bca[:, :], ALU.mult)))(),
                        waits=[(sxr, S.v(sxr))], inc=None)
                    S.op("v", (lambda b=blk: (lambda e: e.tensor_tensor(
                        xlnff[:, b * Tf:(b + 1) * Tf], ft1[:, :], bcb[:, :], ALU.subtract)))(),
                        inc="qv")
                    xe[blk] = S.v("qv")
                    if blk % 8 == 7:
                        yield
                xln_done = S.v("qv")
                for _ in range(6):
                    yield
                gq = {}
                for mi, (mo, mn) in enumerate(MCH):
                    w = [("qv", xln_done)]
                    if mi - 1 in gq:
                        w.append(("qv", gq[mi - 1]))
                    for k in range(BLKS):
                        S.op("t", (lambda k=k, mo=mo, mn=mn, mi=mi: (lambda e: e.matmul(
                            ps[6][0:mn, mi * 8:mi * 8 + NE],
                            xlnff[:, k * Tf + mo:k * Tf + mo + mn],
                            gate_sb[:, k * NE:(k + 1) * NE],
                            start=(k == 0), stop=(k == BLKS - 1))))(),
                            waits=w if k == 0 else (),
                            inc="T" if k == BLKS - 1 else None)
                    S.op("v", (lambda mi=mi, mn=mn: (lambda e: e.max(
                        mx8[0:mn, :], ps[6][0:mn, mi * 8:mi * 8 + NE])))(),
                        waits=[("T", S.v("T"))], inc=None)
                    S.op("v", (lambda mi=mi, mn=mn: (lambda e: e.tensor_tensor(
                        rwcols[0:mn, mi:mi + 1], mx8[0:mn, 0:1], mx8[0:mn, 1:2], ALU.add)))(),
                        inc="qv")
                    gq[mi] = S.v("qv")
                for _ in range(4):
                    yield
                for mi, (mo, mn) in enumerate(MCH):
                    S.op("t", (lambda mi=mi, mn=mn: (lambda e: e.transpose(
                        ps[6][0:1, 0:mn], rwcols[0:mn, mi:mi + 1], ident[0:mn, 0:mn])))(),
                        waits=[("qv", S.v("qv"))], inc="T")
                    S.op("v", (lambda mn=mn: (lambda e: e.tensor_copy(
                        rwr1[0:1, 0:mn], ps[6][0:1, 0:mn])))(),
                        waits=[("T", S.v("T"))], inc="qv")
                    S.op("t", (lambda mi=mi, mo=mo, mn=mn: (lambda e: e.matmul(
                        ps[6][:, mo:mo + mn], onesf[:, :], rwr1[0:1, 0:mn],
                        start=True, stop=True)))(),
                        waits=[("qv", S.v("qv"))], inc="T")
                    S.op("v", (lambda mo=mo, mn=mn: (lambda e: e.tensor_copy(
                        rwbc[:, mo:mo + mn], ps[6][:, mo:mo + mn])))(),
                        waits=[("T", S.v("T"))], inc="qv")
                rw_done = S.v("qv")
                yield

                def evac_fc1(mg, mgo, mgn, nml, tw):
                    for ml in range(nml):
                        sl = mgo // 128 + ml
                        S.op("s", (lambda sl=sl, pt=ps[4 + ml]: (lambda e: e.copy(
                            lat_sb[:, sl * Tf:(sl + 1) * Tf],
                            pt[0:128, 0:Tf])))(),
                            waits=tw if ml == 0 else (),
                            inc="qs" if ml == nml - 1 else None)
                    ws_ps_free["S"] = S.v("qs")
                    ws_ps_free["V"] = S.v("qv")

                yield from ws_gen(fc1b, D, LAT, lambda k: xlnff[:, k * Tf:(k + 1) * Tf],
                                  evac_fc1, extra_w=[("qv", rw_done)])
                lat_done = S.v("qs")

                def evac_exp(mg, mgo, mgn, nml, tw):
                    for ml in range(nml):
                        sl = mgo // 128 + ml
                        S.op("v", (lambda sl=sl, pt=ps[4 + ml]: (lambda e: e.tensor_tensor(
                            moe_sb[:, sl * Tf:(sl + 1) * Tf],
                            pt[0:128, 0:Tf], rwbc[:, :], ALU.mult)))(),
                            waits=tw if ml == 0 else (),
                            inc="qv" if ml == nml - 1 else None)
                    ws_ps_free["S"] = S.v("qs")
                    ws_ps_free["V"] = S.v("qv")

                yield from ws_gen(expb, LAT, LAT, lambda k: lat_sb[:, k * Tf:(k + 1) * Tf],
                                  evac_exp, extra_w=[("qs", lat_done), ("qv", rw_done)])
                moe_done = S.v("qv")

                def evac_up(mg, mgo, mgn, nml, tw):
                    ufx = upst[0]
                    upst[0] += 1
                    hb = ufx % 2
                    shs = f"hs{hb}"
                    w = list(tw)
                    if ufx - 2 in hio_free:
                        w.append((shs, hio_free[ufx - 2]))
                    for ml in range(nml):
                        fi = ftrot[0] % 2
                        ftrot[0] += 1
                        fa = fu[fi]
                        sw = list(w) if ml == 0 else []
                        if fi in fu_free:
                            sw.append(("qv", fu_free[fi]))
                        S.op("s", (lambda fa=fa, pt=ps[4 + ml]: (lambda e: e.activation(
                            fa[:, :], pt[0:128, 0:Tf], AF.Relu)))(),
                            waits=sw, inc="qs")
                        S.op("v", (lambda fa=fa, ml=ml, hb=hb, pt=ps[4 + ml]: (lambda e: e.tensor_tensor(
                            hio[hb][:, ml * Tf:(ml + 1) * Tf], fa[:, :],
                            pt[0:128, 0:Tf], ALU.mult)))(),
                            waits=[("qs", S.v("qs"))], inc="qv")
                        fu_free[fi] = S.v("qv")
                    ws_ps_free["S"] = S.v("qs")
                    ws_ps_free["V"] = S.v("qv")
                    stw = [("qv", S.v("qv"))]
                    if pss > 0:
                        stw += [("hl0", h_rd[pss - 1][0]), ("hl1", h_rd[pss - 1][1])]
                    S.op("y", (lambda mgo=mgo, mgn=mgn, hb=hb, nml=nml: (lambda e: e.dma_start(
                        out=h_h[mgo:mgo + mgn, :].rearrange(
                            "(m p) t -> p m t", p=128),
                        in_=hio[hb][:, 0:nml * Tf])))(),
                        waits=stw, inc=shs, dma=True)
                    hio_free[ufx] = S.v(shs)

                yield from ws_gen(upb, D, INT, lambda k: xlnff[:, k * Tf:(k + 1) * Tf],
                                  evac_up, extra_w=[("qv", moe_done)])
                up_done = [("hs0", S.v("hs0")), ("hs1", S.v("hs1"))]

                sdone = {}
                hl_free = {}
                for mg, (mgo, mgn) in enumerate(nsplit(D, MGW)):
                    nml = (mgn + 127) // 128
                    rch = nsplit(INT, KROW)
                    for ri, (ro, rn) in enumerate(rch):
                        wi = wbi[0] % 2
                        wbi[0] += 1
                        sld = f"wl{wi}"
                        fidx = mg * len(rch) + ri
                        hb = fidx % 2
                        shl = f"hl{hb}"
                        w = list(up_done) + [("ld0", st["precast"])]
                        if wi in wb_free:
                            w.append(("T", wb_free[wi]))
                        S.op("y", (lambda ro=ro, rn=rn, mgo=mgo, mgn=mgn, wi=wi: (lambda e: e.dma_start(
                            out=wb[wi][:, 0:(rn // 128) * mgn],
                            in_=downb[ro:ro + rn, mgo:mgo + mgn].rearrange(
                                "(k p) c -> p k c", p=128))))(),
                            waits=w, inc=sld, dma=True)
                        hw = list(up_done)
                        if fidx - 2 in hl_free:
                            hw.append(("T", hl_free[fidx - 2]))
                        S.op("y", (lambda ro=ro, rn=rn, hb=hb: (lambda e: e.dma_start(
                            out=hio[hb][:, 0:(rn // 128) * Tf],
                            in_=h_h[ro:ro + rn, :].rearrange("(k p) t -> p k t", p=128))))(),
                            waits=hw, inc=shl, dma=True)
                        lw = [(sld, S.v(sld)), (shl, S.v(shl))]
                        if ri == 0 and mg >= 1:
                            lw.append(("qs", sdone[mg - 1]))
                        for k in range(rn // 128):
                            for ml in range(nml):
                                S.op("t", (lambda wi=wi, hb=hb, k=k, ml=ml, mgn=mgn,
                                            pt=ps[4 + ml],
                                            first=(ri == 0 and k == 0): (lambda e: e.matmul(
                                    pt[0:128, 0:Tf],
                                    wb[wi][:, k * mgn + ml * 128:k * mgn + ml * 128 + 128],
                                    hio[hb][:, k * Tf:(k + 1) * Tf],
                                    start=first, stop=False)))(),
                                    waits=lw if (k == 0 and ml == 0) else (),
                                    inc="T" if (k == rn // 128 - 1 and ml == nml - 1) else None)
                        wb_free[wi] = S.v("T")
                        hl_free[fidx] = S.v("T")
                        yield
                    rch2 = nsplit(LAT, KROW)
                    for ri2, (ro2, rn2) in enumerate(rch2):
                        wi = wbi[0] % 2
                        wbi[0] += 1
                        sld = f"wl{wi}"
                        w = [("qv", moe_done), ("ld0", st["precast"])]
                        if wi in wb_free:
                            w.append(("T", wb_free[wi]))
                        S.op("y", (lambda ro2=ro2, rn2=rn2, mgo=mgo, mgn=mgn, wi=wi: (lambda e: e.dma_start(
                            out=wb[wi][:, 0:(rn2 // 128) * mgn],
                            in_=fc2b[ro2:ro2 + rn2, mgo:mgo + mgn].rearrange("(k p) c -> p k c", p=128))))(),
                            waits=w, inc=sld, dma=True)
                        lw = [(sld, S.v(sld))]
                        for k in range(rn2 // 128):
                            for ml in range(nml):
                                S.op("t", (lambda wi=wi, k=k, kk=ro2 // 128 + k, ml=ml, mgn=mgn,
                                            pt=ps[4 + ml],
                                            last=(ri2 == len(rch2) - 1 and k == rn2 // 128 - 1): (lambda e: e.matmul(
                                    pt[0:128, 0:Tf],
                                    wb[wi][:, k * mgn + ml * 128:k * mgn + ml * 128 + 128],
                                    moe_sb[:, kk * Tf:(kk + 1) * Tf],
                                    start=False, stop=last)))(),
                                    waits=lw if (k == 0 and ml == 0) else (),
                                    inc="T" if (k == rn2 // 128 - 1 and ml == nml - 1) else None)
                        wb_free[wi] = S.v("T")
                        yield
                    tw = [("T", S.v("T"))]
                    for ml in range(nml):
                        sl = mgo // 128 + ml
                        S.op("s", (lambda sl=sl, pt=ps[4 + ml]: (lambda e: e.activation(
                            s_sb[:, sl * Tf:(sl + 1) * Tf],
                            pt[0:128, 0:Tf], AF.Copy)))(),
                            waits=tw if ml == 0 else (),
                            inc="qs" if ml == nml - 1 else None)
                    sdone[mg] = S.v("qs")
                    ws_ps_free["S"] = S.v("qs")
                    ws_ps_free["V"] = S.v("qv")
                    yield
                h_rd[pss] = (S.v("hl0"), S.v("hl1"))
                s_done = S.v("qs")

                st3 = {}
                c3 = {}

                def ln3_mm(blk):
                    par = blk % 2
                    sw = [("qs", c3[blk])]
                    S.op("t", (lambda p=par, b=blk, hc=hc: (lambda e: e.matmul(
                        ps[7][64:65, hc:hc + Tf], onesbf[:, :], sbf3[p][:, :],
                        start=(b == 0), stop=(b == BLKS - 1),
                        tile_position=(0, 64))))(),
                        waits=sw, inc=None)
                    S.op("t", (lambda p=par, b=blk, hc=hc: (lambda e: e.matmul(
                        ps[7][96:97, hc:hc + Tf], onesbf[:, :], ssq3[p][:, :],
                        start=(b == 0), stop=(b == BLKS - 1),
                        tile_position=(0, 96))))(), inc="T")
                    st3[blk] = S.v("T")

                for blk in range(BLKS):
                    par = blk % 2
                    w = [("qs", s_done)]
                    if blk - 2 in st3:
                        w.append(("T", st3[blk - 2]))
                    S.op("s", (lambda p=par, b=blk: (lambda e: e.copy(
                        sbf3[p][:, :], s_sb[:, b * Tf:(b + 1) * Tf])))(),
                        waits=w, inc=None)
                    S.op("s", (lambda p=par: (lambda e: e.square(
                        ssq3[p][:, :], sbf3[p][:, :])))(), inc="qs")
                    c3[blk] = S.v("qs")
                    if blk >= 1:
                        ln3_mm(blk - 1)
                    if blk % 4 == 3:
                        yield
                ln3_mm(BLKS - 1)
                yield
                rows_chain(ps[7][64:65, hc:hc + Tf], ps[7][96:97, hc:hc + Tf],
                           [("T", S.v("T"))])
                for _ in range(5):
                    yield
                bc_pair()
                yield
                for blk in range(BLKS):
                    par = blk % 2
                    sxr, sot = f"xr{par}", f"ot{par}"
                    w = [("qv", S.v("qv"))]
                    if (pss, blk - 2) in odone:
                        w.append((sot, odone[(pss, blk - 2)]))
                    elif pss > 0:
                        w.append((sot, odone[(pss - 1, BLKS - 2 + par)]))
                    S.op("y", (lambda p=par, b=blk, pc=pc: (lambda e: e.dma_start(
                        out=x1r[p][:, :], in_=x1_h[b * 128:(b + 1) * 128, pc:pc + Tf])))(),
                        waits=w, inc=sxr, dma=True)
                    vw = [(sxr, S.v(sxr))]
                    if (pss, blk - 2) in odone:
                        vw.append((sot, odone[(pss, blk - 2)]))
                    elif pss > 0:
                        vw.append((sot, odone[(pss - 1, BLKS - 2 + par)]))
                    S.op("v", (lambda b=blk: (lambda e: e.tensor_tensor(
                        ft1[:, :], s_sb[:, b * Tf:(b + 1) * Tf], bca[:, :], ALU.mult)))(),
                        waits=vw, inc=None)
                    S.op("v", lambda e: e.tensor_tensor(
                        ft2[:, :], ft1[:, :], bcb[:, :], ALU.subtract), inc=None)
                    S.op("v", (lambda b=blk: (lambda e: e.tensor_scalar(
                        ft1[:, :], ft2[:, :], gam_sb[:, b:b + 1], bet_sb[:, b:b + 1],
                        ALU.mult, ALU.add)))(), inc=None)
                    S.op("v", (lambda p=par: (lambda e: e.tensor_tensor(
                        ob[p][:, :], ft1[:, :], x1r[p][:, :], ALU.add)))(), inc="qv")
                    S.op("y", (lambda p=par, b=blk, pc=pc: (lambda e: e.dma_start(
                        out=out_dt[b * 128:(b + 1) * 128, pc:pc + Tf], in_=ob[p][:, :])))(),
                        waits=[("qv", S.v("qv"))], inc=sot, dma=True)
                    odone[(pss, blk)] = S.v(sot)
                    if blk % 8 == 7:
                        yield

            for c in range(NP):
                p1_subA(c)
                if c == 0:
                    sched_precast()
                for blk in range(BLKS):
                    p1_subB_blk(c, blk)
                    p2_stats_blk(c, blk)
                    if blk >= 1:
                        if blk == 1:
                            while c >= 1 and (c - 1) not in ln2_read:
                                pump(5)
                        p2_stats_mm(c, blk - 1)
                    pump(1)
                p2_stats_mm(c, BLKS - 1)
                pending.append(p2_pass_gen(c, dict(x1_done)))
            while pending:
                pump(100)

            S.op("y", lambda e: e.nop(),
                 waits=[("ot0", S.v("ot0")), ("ot1", S.v("ot1"))])
            S.emit()
    return nc



def _stage_core(cfg, inputs, b, half):
    D, T = cfg.D, cfg.T
    BLKS, TC, Tf, G4 = cfg.BLKS, cfg.TC, cfg.Tf, cfg.G4
    f32 = np.float32
    x = np.asarray(inputs["L0_attn_x"][b], f32)
    x_dt = np.ascontiguousarray(x.T)
    tok = np.concatenate([np.arange(p * TC + half * Tf, p * TC + half * Tf + Tf)
                          for p in range(cfg.NPASS)])
    xcols = np.ascontiguousarray(x_dt[:, tok])

    A_log = np.asarray(inputs["L0_attn_A_log"], f32)
    Ar = A_log.reshape(D // 32, 32, 4, 4)
    alog_il = np.ascontiguousarray(
        Ar.transpose(1, 3, 0, 2).reshape(128, G4))

    cm = lambda v: np.ascontiguousarray(np.asarray(v, f32).reshape(BLKS, 128).T)
    r4 = np.zeros((128, 32), f32)
    for p in range(128):
        r4[p, p // 4] = 1.0

    z = np.zeros((128, 1), f32)
    return {
        "x_dt": x_dt,
        "xcols": xcols,
        "up_w": np.ascontiguousarray(inputs["L0_ffn_up_w"], f32),
        "down_w": np.ascontiguousarray(inputs["L0_ffn_down_w"], f32),
        "fc1_w": np.ascontiguousarray(inputs["L0_ffn_fc1_w"], f32),
        "fc2_w": np.ascontiguousarray(inputs["L0_ffn_fc2_w"], f32),
        "expert_w": np.ascontiguousarray(inputs["L0_ffn_expert_w"], f32),
        "gate_w": np.ascontiguousarray(inputs["L0_ffn_gate_w"], f32),
        "xpw_t": np.ascontiguousarray(np.asarray(inputs["L0_attn_x_proj_w"], f32).T),
        "dtw_t": np.ascontiguousarray(np.asarray(inputs["L0_attn_dt_proj_w"], f32).T),
        "alog_il": alog_il,
        "b_cm": cm(inputs["L0_attn_dt_proj_b"]),
        "d_cm": cm(inputs["L0_attn_D"]),
        "gam_cm": cm(inputs["L0_ffn_ln_gamma"]),
        "bet_cm": cm(inputs["L0_ffn_ln_beta"]),
        "r4": r4.astype(ml_dtypes.bfloat16),
        "ones_bf": np.ones((128, 1), ml_dtypes.bfloat16),
        "ones_f": np.ones((1, 128), f32),
        "ident_f": np.eye(128, dtype=f32),
        "mask_a": z + (0.0 if half else 1.0),
        "mask_b": z + (1.0 if half else 0.0),
    }, tok


def kernel(**inputs):
    cfg = CFG
    nc = build_core_program(cfg)
    in_maps, toks = [], []
    for k in range(8):
        im, tok = _stage_core(cfg, inputs, k // 2, k % 2)
        in_maps.append(im)
        toks.append(tok)
    res = run_bass_kernel_spmd(nc, in_maps, core_ids=list(range(8)))
    B = inputs["L0_attn_x"].shape[0]
    out = np.empty((B, cfg.T, cfg.D), np.float32)
    for k in range(8):
        out[k // 2, toks[k], :] = np.asarray(res.results[k]["out"]).T
    return out


# version 24
# speedup vs baseline: 1.1335x; 1.1335x over previous
from contextlib import ExitStack
from dataclasses import dataclass

import numpy as np
import ml_dtypes
import concourse.bass as bass
import concourse.mybir as mybir
from concourse.bass_utils import run_bass_kernel_spmd

F32 = mybir.dt.float32
BF16 = mybir.dt.bfloat16
AF = mybir.ActivationFunctionType
ALU = mybir.AluOpType


@dataclass(frozen=True)
class Cfg:
    D: int = 4096
    T: int = 2048
    R: int = 128
    INT: int = 8192
    LAT: int = 2048
    NE: int = 8
    NPASS: int = 4
    eps: float = 1e-5

    @property
    def BLKS(self):
        return self.D // 128

    @property
    def TC(self):
        return self.T // self.NPASS

    @property
    def Tf(self):
        return self.TC // 2

    @property
    def TOKS(self):
        return self.T // 2

    @property
    def G4(self):
        return (self.D // 32) * 4


CFG = Cfg()
SEMS = ("ld0", "ld1", "xa0", "xa1", "xb0", "xb1", "rp0", "rp1", "bc", "xst",
        "yh0", "yh1", "xy0", "xy1", "xs0", "xs1", "xr0", "xr1",
        "hs0", "hs1", "hl0", "hl1", "ot0", "ot1", "T",
        "wl0", "wl1", "xy2", "xy3", "xs2", "xs3", "qv", "qs", "qg")
AUTOSYNC = {"v": "qv", "s": "qs", "g": "qg"}
KROW = 1024
DEBUG_DUMP = False
STRICT_SYNC = True


def nsplit(total, step=512):
    return [(i, min(step, total - i)) for i in range(0, total, step)]


class Sched:
    def __init__(self, nc, sems, cnt):
        self.nc, self.sems, self.cnt = nc, sems, cnt
        self.prog = []

    def op(self, eng, fn, waits=(), inc=None, dma=False):
        incs = []
        qn = AUTOSYNC.get(eng)
        if qn is not None and not dma:
            assert inc is None or inc == qn, (eng, inc)
            self.cnt[qn] += 1
            incs = [(qn, 1)]
        elif inc is not None:
            n = 16 if dma else 1
            self.cnt[inc] += n
            incs = [(inc, n)]
        self.prog.append((eng, tuple(waits), fn, incs, dma))

    def v(self, s):
        return self.cnt[s]

    def snap(self):
        return [(s, self.cnt[s]) for s in SEMS if self.cnt[s] > 0]

    def barrier(self):
        w = self.snap()
        for e in ("g", "v", "s", "t", "y"):
            self.prog.append((e, tuple(w), lambda en: en.nop(), [], True))

    def emit(self):
        with self.nc.Block() as block:
            def mk(key):
                def emitter(eng):
                    last = {}
                    qs = AUTOSYNC.get(key)
                    for (e, waits, fn, incs, is_dma) in self.prog:
                        if e != key:
                            continue
                        for (s, val) in waits:
                            if s == qs and not is_dma and not STRICT_SYNC:
                                continue
                            if val > 0 and last.get(s, 0) < val:
                                eng.wait_ge(self.sems[s], val)
                                last[s] = val
                        if STRICT_SYNC and qs is not None and not is_dma:
                            nq = self.cnt.get("em_" + qs, 0)
                            if nq > 0 and last.get(qs, 0) < nq:
                                eng.wait_ge(self.sems[qs], nq)
                                last[qs] = nq
                        instr = fn(eng)
                        for (s, n) in incs:
                            instr.then_inc(self.sems[s], n)
                            if s == qs:
                                self.cnt["em_" + qs] = self.cnt.get("em_" + qs, 0) + 1
                return emitter
            block.gpsimd(mk("g"))
            block.vector(mk("v"))
            block.scalar(mk("s"))
            block.tensor(mk("t"))
            block.sync(mk("y"))


def build_core_program(cfg: Cfg):
    nc = bass.Bass("TRN2", target_bir_lowering=False, debug=False)
    D, T, R, INT, LAT, NE = cfg.D, cfg.T, cfg.R, cfg.INT, cfg.LAT, cfg.NE
    BLKS, TC, Tf, TOKS, NP, G4 = cfg.BLKS, cfg.TC, cfg.Tf, cfg.TOKS, cfg.NPASS, cfg.G4
    RW = R + 32
    assert TC <= 512 and Tf * 2 == TC

    def dr(name, sh, dt=F32, kind=None):
        t = nc.dram_tensor(name, list(sh), dt, kind=kind) if kind else \
            nc.dram_tensor(name, list(sh), dt)
        return t.ap()

    x_dt = dr("x_dt", (D, T), kind="ExternalInput")
    xcols = dr("xcols", (D, TOKS), kind="ExternalInput")
    up_w = dr("up_w", (D, INT), kind="ExternalInput")
    down_w = dr("down_w", (INT, D), kind="ExternalInput")
    fc1_w = dr("fc1_w", (D, LAT), kind="ExternalInput")
    fc2_w = dr("fc2_w", (LAT, D), kind="ExternalInput")
    expert_w = dr("expert_w", (LAT, LAT), kind="ExternalInput")
    gate_w = dr("gate_w", (D, NE), kind="ExternalInput")
    xpw_t = dr("xpw_t", (D, RW), kind="ExternalInput")
    dtw_t = dr("dtw_t", (R, D), kind="ExternalInput")
    alog_il = dr("alog_il", (128, G4), kind="ExternalInput")
    b_cm = dr("b_cm", (128, BLKS), kind="ExternalInput")
    d_cm = dr("d_cm", (128, BLKS), kind="ExternalInput")
    gam_cm = dr("gam_cm", (128, BLKS), kind="ExternalInput")
    bet_cm = dr("bet_cm", (128, BLKS), kind="ExternalInput")
    r4_in = dr("r4", (128, 32), BF16, kind="ExternalInput")
    onesbf_in = dr("ones_bf", (128, 1), BF16, kind="ExternalInput")
    onesf_in = dr("ones_f", (1, 128), kind="ExternalInput")
    ident_in = dr("ident_f", (128, 128), kind="ExternalInput")
    mA_in = dr("mask_a", (128, 1), kind="ExternalInput")
    mB_in = dr("mask_b", (128, 1), kind="ExternalInput")
    out_dt = dr("out", (D, TOKS), kind="ExternalOutput")
    y_h = dr("y_h", (D, T))
    upb = dr("upb", (D, INT), BF16)
    downb = dr("downb", (INT, D), BF16)
    fc1b = dr("fc1b", (D, LAT), BF16)
    fc2b = dr("fc2b", (LAT, D), BF16)
    expb = dr("expb", (LAT, LAT), BF16)
    x1_h = dr("x1_h", (D, TOKS))
    h_h = dr("h_h", (INT, Tf), BF16)
    xpB_h = dr("xpB_h", (32, TC), BF16)

    with ExitStack() as big:
        sb = lambda n, sh, dt=F32: big.enter_context(nc.sbuf_tensor(n, list(sh), dt))
        sems, cnt = {}, {}
        for s in SEMS:
            sems[s] = big.enter_context(nc.semaphore(s))
            cnt[s] = 0
        ps = [big.enter_context(nc.psum_tensor(f"ps{i}", [128, 512], F32))
              for i in range(8)]

        gate_sb = sb("gate_sb", (128, BLKS * NE), BF16)
        gam_sb = sb("gam_sb", (128, BLKS))
        bet_sb = sb("bet_sb", (128, BLKS))
        r4_sb = sb("r4_sb", (128, 32), BF16)
        onesbf = sb("onesbf", (128, 1), BF16)
        onesf = sb("onesf", (1, 128))
        ident = sb("ident", (128, 128))
        mA_sb = sb("mA_sb", (128, 1))
        mB_sb = sb("mB_sb", (128, 1))
        epscol = sb("epscol", (128, 1))
        xpw_sb = sb("xpw_sb", (128, BLKS * RW), BF16)
        dtw_sb = sb("dtw_sb", (R, D), BF16)
        A_il = sb("A_il", (128, G4))
        bcol = sb("bcol", (128, BLKS))
        dcol = sb("dcol", (128, BLKS))
        rbc = sb("rbc", (128, T))
        mrbc = sb("mrbc", (128, T))
        states = sb("states", (128, G4))

        S0 = Sched(nc, sems, cnt)
        with ExitStack() as p0:
            p0sb = lambda n, sh, dt=F32: p0.enter_context(
                nc.sbuf_tensor(n, list(sh), dt))
            xst = [p0sb(f"xst{i}", (128, T)) for i in range(2)]
            xbf = [p0sb(f"xbf{i}", (128, T), BF16) for i in range(2)]
            xsq = [p0sb(f"xsq{i}", (128, T), BF16) for i in range(2)]
            atmp = p0sb("atmp", (128, G4))
            rows4 = p0sb("rows4", (1, 4 * T))
            mrow = rows4.ap()[:, 0:T]
            e2row = rows4.ap()[:, T:2 * T]
            rrow = rows4.ap()[:, 2 * T:3 * T]
            trow = rows4.ap()[:, 3 * T:4 * T]

            for dst, src in [
                (xpw_sb, xpw_t.rearrange("(b p) r -> p b r", p=128)),
                (dtw_sb, dtw_t),
                (gate_sb, gate_w.rearrange("(b p) e -> p b e", p=128)),
                (A_il, alog_il), (bcol, b_cm), (dcol, d_cm),
                (gam_sb, gam_cm), (bet_sb, bet_cm), (r4_sb, r4_in),
                (onesbf, onesbf_in), (onesf, onesf_in), (ident, ident_in),
                (mA_sb, mA_in), (mB_sb, mB_in),
            ]:
                S0.op("g", (lambda d=dst, s_=src: (lambda e: e.dma_start(
                    out=d[:, :], in_=s_)))(), inc="ld0", dma=True)
            ld0 = S0.v("ld0")
            S0.op("s", lambda e: e.activation(atmp[:, :], A_il[:, :], AF.Exp),
                  waits=[("ld0", ld0)], inc="qs")
            S0.op("v", lambda e: e.tensor_scalar(
                A_il[:, :], atmp[:, :], -1.0, None, ALU.mult),
                waits=[("qs", S0.v("qs"))], inc="qv")
            S0.op("v", lambda e: e.memset(states[:, :], 0.0), inc="qv")
            S0.op("v", lambda e: e.memset(epscol[:, :], cfg.eps), inc="qv")

            NCHT = nsplit(T)
            assert len(NCHT) <= 4
            sfree = {}
            for blk in range(BLKS):
                p = blk % 2
                sx = f"xa{p}"
                w = [("ld0", ld0)] if blk < 2 else [("T", sfree[blk - 2])]
                S0.op("y", (lambda p=p, b=blk: (lambda e: e.dma_start(
                    out=xst[p][:, :], in_=x_dt[b * 128:(b + 1) * 128, :])))(),
                    waits=w, inc=sx, dma=True)
                S0.op("s", (lambda p=p: (lambda e: e.copy(
                    xbf[p][:, :], xst[p][:, :])))(),
                    waits=[(sx, S0.v(sx))], inc=None)
                S0.op("s", (lambda p=p: (lambda e: e.square(
                    xsq[p][:, :], xbf[p][:, :])))(), inc="qs")
                sw = [("qs", S0.v("qs"))]
                for i, (o, n) in enumerate(NCHT):
                    S0.op("t", (lambda p=p, i=i, o=o, n=n, b=blk: (lambda e: e.matmul(
                        ps[i][0:1, 0:n], onesbf[:, :], xbf[p][:, o:o + n],
                        start=(b == 0), stop=(b == BLKS - 1))))(),
                        waits=sw if i == 0 else (), inc=None)
                    S0.op("t", (lambda p=p, i=i, o=o, n=n, b=blk: (lambda e: e.matmul(
                        ps[4 + i][0:1, 0:n], onesbf[:, :], xsq[p][:, o:o + n],
                        start=(b == 0), stop=(b == BLKS - 1))))(),
                        inc="T" if i == len(NCHT) - 1 else None)
                sfree[blk] = S0.v("T")
            tw = [("T", S0.v("T"))]
            for i, (o, n) in enumerate(NCHT):
                S0.op("s", (lambda i=i, o=o, n=n: (lambda e: e.activation(
                    mrow[0:1, o:o + n], ps[i][0:1, 0:n], AF.Copy, scale=1.0 / D)))(),
                    waits=tw if i == 0 else (), inc=None)
                S0.op("s", (lambda i=i, o=o, n=n: (lambda e: e.activation(
                    e2row[0:1, o:o + n], ps[4 + i][0:1, 0:n], AF.Copy, scale=1.0 / D)))(),
                    inc="qs" if i == len(NCHT) - 1 else None)
            S0.op("v", lambda e: e.tensor_tensor(
                trow[:, :], mrow[:, :], mrow[:, :], ALU.mult),
                waits=[("qs", S0.v("qs"))], inc=None)
            S0.op("v", lambda e: e.tensor_tensor(
                trow[:, :], e2row[:, :], trow[:, :], ALU.subtract), inc="qv")
            S0.op("s", lambda e: e.activation(
                rrow[:, :], trow[:, :], AF.Ln, bias=epscol[0:1, 0:1]),
                waits=[("qv", S0.v("qv"))], inc=None)
            S0.op("s", lambda e: e.activation(
                rrow[:, :], rrow[:, :], AF.Exp, scale=-0.5), inc="qs")
            S0.op("v", lambda e: e.tensor_tensor(
                trow[:, :], mrow[:, :], rrow[:, :], ALU.mult),
                waits=[("qs", S0.v("qs"))], inc="qv")
            bw = [("qv", S0.v("qv"))]
            for i, (o, n) in enumerate(NCHT):
                S0.op("t", (lambda o=o, n=n: (lambda e: e.matmul(
                    ps[0][:, 0:n], onesf[:, :], rrow[0:1, o:o + n],
                    start=True, stop=True)))(), waits=bw, inc="T")
                S0.op("v", (lambda o=o, n=n: (lambda e: e.tensor_copy(
                    rbc[:, o:o + n], ps[0][:, 0:n])))(),
                    waits=[("T", S0.v("T"))], inc="qv")
                S0.op("t", (lambda o=o, n=n: (lambda e: e.matmul(
                    ps[1][:, 0:n], onesf[:, :], trow[0:1, o:o + n],
                    start=True, stop=True)))(),
                    waits=[("qv", S0.v("qv"))], inc="T")
                S0.op("v", (lambda o=o, n=n: (lambda e: e.tensor_copy(
                    mrbc[:, o:o + n], ps[1][:, 0:n])))(),
                    waits=[("T", S0.v("T"))], inc="qv")
                bw = [("qv", S0.v("qv"))]
            S0.barrier()
            S0.emit()

        S = Sched(nc, sems, cnt)
        with ExitStack() as mg_:
            m_ = lambda n, sh, dt=F32: mg_.enter_context(nc.sbuf_tensor(n, list(sh), dt))
            xs1 = [m_(f"xs1_{i}", (128, TC)) for i in range(2)]
            xs2 = [m_(f"xs2_{i}", (128, TC)) for i in range(2)]
            tmpa = m_("tmpa", (128, TC))
            tmpb = tmpa
            xlnbf = [m_(f"xlnbf{i}", (128, TC), BF16) for i in range(2)]
            dtin = m_("dtin", (R, TC), BF16)
            xpB = m_("xpB", (32, TC), BF16)
            es = m_("es", (128, TC))
            dtb = [m_(f"dtb{i}", (128, TC)) for i in range(2)]
            xln2 = [m_(f"xln2_{i}", (128, TC)) for i in range(2)]
            ub = [m_(f"ub{i}", (128, TC), BF16) for i in range(2)]
            dtrep = [m_(f"dtrep{i}", (128, TC)) for i in range(2)]
            urep = [m_(f"urep{i}", (128, TC), BF16) for i in range(2)]
            BrepB = [m_(f"Brep{i}", (128, TC), BF16) for i in range(4)]
            CrepB = [m_(f"Crep{i}", (128, TC), BF16) for i in range(4)]
            NDA = 5
            dAb = [m_(f"dA{i}", (128, TC)) for i in range(NDA)]
            dBxb = [m_(f"dBx{i}", (128, TC), BF16) for i in range(NDA)]
            HSD = 7
            h_sb = [m_(f"h_sb{i}", (128, TC), BF16) for i in range(HSD)]
            NHC = 7
            hCb = [m_(f"hC{i}", (128, TC), BF16) for i in range(NHC)]
            yb0 = m_("yb0", (128, TC))
            yb = [yb0, yb0]
            arenaA = m_("arenaA", (128, BLKS * Tf))
            xlnff = arenaA.ap().bitcast(BF16)
            s_sb = arenaA
            lat_sb = arenaA.ap().bitcast(BF16)[:, BLKS * Tf:BLKS * Tf + (LAT // 128) * Tf]
            moe_sb = m_("moe_sb", (128, (LAT // 128) * Tf), BF16)
            NKW = KROW // 128
            wb = [m_(f"wb{i}", (128, NKW * 512), BF16) for i in range(2)]
            hio = [m_(f"hio{i}", (128, NKW * Tf), BF16) for i in range(2)]
            xa = [m_(f"xa{i}", (128, Tf)) for i in range(3)]
            yba = [m_(f"yba{i}", (128, Tf)) for i in range(3)]
            ybb = [m_(f"ybb{i}", (128, Tf)) for i in range(3)]
            x1b = [m_(f"x1b{i}", (128, Tf)) for i in range(3)]
            x1r = [m_(f"x1r{i}", (128, Tf)) for i in range(2)]
            ob = [m_(f"ob{i}", (128, Tf)) for i in range(2)]
            ft1 = m_("ft1", (128, Tf))
            ft2 = m_("ft2", (128, Tf))
            ftst = m_("ftst", (128, Tf))
            fu = [m_(f"fu{i}", (128, Tf)) for i in range(2)]
            bca = m_("bca", (128, Tf))
            bcb = m_("bcb", (128, Tf))
            sbf = [m_(f"sbf{i}", (128, Tf), BF16) for i in range(3)]
            ssq = [m_(f"ssq{i}", (128, Tf), BF16) for i in range(3)]
            sbf3 = [m_(f"sbf3_{i}", (128, Tf), BF16) for i in range(2)]
            ssq3 = [m_(f"ssq3_{i}", (128, Tf), BF16) for i in range(2)]
            MCH = nsplit(Tf, 128)
            rwcols = m_("rwcols", (128, len(MCH)))
            mx8 = m_("mx8", (128, 8))
            rwbc = m_("rwbc", (128, Tf), BF16)
            m2r = m_("m2r", (1, Tf))
            e2r = m_("e2r", (1, Tf))
            r2r = e2r
            t2r = m_("t2r", (1, Tf))
            rwr1 = t2r
            assert Tf <= 256 and len(MCH) <= 2

            xa_free = {}
            yps_free = {}
            ybl_free = {}
            xln_free = {}
            xp_mm = {}
            dtps_free = {}
            rep_free = {}
            dtb_cons = {}
            cons = {}
            pend_stc = {}
            ud = {}
            ypstop = {}
            st = {"xp_evac": 0, "bc_done": 0}
            psdt = [ps[0], ps[1]]
            psy = [ps[2], ps[3]]
            wbi = [0]
            ftrot = [0]
            upst = [0]
            fu_free = {}
            wb_free = {}
            hio_free = {}
            ws_ps_free = {}
            x1_done = {}
            x1c = {}
            odone = {}
            stat_done = {}
            ln2_read = {}
            h_rd = {}
            MGW = 384
            NMG_UP = len(nsplit(INT, MGW))

            pending = []

            def pump(n):
                done = 0
                while pending and done < n:
                    try:
                        next(pending[0])
                        done += 1
                    except StopIteration:
                        pending.pop(0)

            def sched_precast():
                first = [("xa0", S.v("xa0")), ("xa1", S.v("xa1"))]
                for wt_d, wt_s, krows in [(fc1b, fc1_w, D), (expb, expert_w, LAT),
                                          (upb, up_w, D), (downb, down_w, INT),
                                          (fc2b, fc2_w, LAT)]:
                    for (ro, rn) in nsplit(krows, 1024):
                        S.op("g", (lambda d=wt_d, s_=wt_s, ro=ro, rn=rn: (lambda e: e.dma_start(
                            out=d[ro:ro + rn, :], in_=s_[ro:ro + rn, :])))(),
                            waits=first, inc="ld0", dma=True)
                        first = []
                st["precast"] = S.v("ld0")

            def p1_subA(c):
                co = c * TC
                for blk in range(BLKS):
                    p = blk % 2
                    sx = f"xa{p}"
                    if (c, blk - 2) in xa_free:
                        w = [("qs", xa_free[(c, blk - 2)])]
                    elif c > 0:
                        w = [("qs", xa_free[(c - 1, BLKS - 2 + blk)])]
                    else:
                        w = []
                    S.op("y", (lambda p=p, b=blk, co=co: (lambda e: e.dma_start(
                        out=xs1[p][:, :],
                        in_=x_dt[b * 128:(b + 1) * 128, co:co + TC])))(),
                        waits=w, inc=sx, dma=True)
                    S.op("v", (lambda p=p, co=co: (lambda e: e.tensor_tensor(
                        tmpa[:, :], xs1[p][:, :], rbc[:, co:co + TC], ALU.mult)))(),
                        waits=[(sx, S.v(sx))], inc=None)
                    S.op("v", (lambda p=p, co=co: (lambda e: e.tensor_tensor(
                        xs1[p][:, :], tmpa[:, :], mrbc[:, co:co + TC], ALU.subtract)))(),
                        inc="qv")
                    cw = [("qv", S.v("qv"))]
                    if (c, blk - 2) in xp_mm:
                        cw.append(("T", xp_mm[(c, blk - 2)]))
                    elif c > 0:
                        cw.append(("T", xp_mm[(c - 1, BLKS - 2 + blk)]))
                    S.op("s", (lambda p=p: (lambda e: e.copy(
                        xlnbf[p][:, :], xs1[p][:, :])))(),
                        waits=cw, inc="qs")
                    xa_free[(c, blk)] = S.v("qs")
                    sw = [("qs", S.v("qs"))]
                    if blk == 0:
                        sw += [("qg", S.v("qg")), ("qv", S.v("qv"))]
                    S.op("t", (lambda p=p, b=blk: (lambda e: e.matmul(
                        ps[0][:, 0:TC], xpw_sb[:, b * RW:b * RW + 128],
                        xlnbf[p][:, :],
                        start=(b == 0), stop=(b == BLKS - 1))))(),
                        waits=sw, inc=None)
                    S.op("t", (lambda p=p, b=blk: (lambda e: e.matmul(
                        ps[1][0:32, 0:TC],
                        xpw_sb[:, b * RW + 128:(b + 1) * RW],
                        xlnbf[p][:, :],
                        start=(b == 0), stop=(b == BLKS - 1))))(),
                        inc="T")
                    xp_mm[(c, blk)] = S.v("T")
                    pump(1)
                tw = [("T", S.v("T"))]
                S.op("v", lambda e: e.tensor_copy(dtin[:, :], ps[0][0:R, 0:TC]),
                     waits=tw, inc=None)
                S.op("v", lambda e: e.tensor_copy(xpB[:, :], ps[1][0:32, 0:TC]),
                     inc="qv")
                st["xp_evac"] = S.v("qv")
                w = [("qv", st["xp_evac"])]
                if c > 0:
                    w.append(("bc", st["bc_done"]))
                S.op("y", lambda e: e.dma_start(out=xpB_h, in_=xpB[:, :]),
                     waits=w, inc="xst", dma=True)
                xpB_st = S.v("xst")
                for q in range(4):
                    S.op("s", (lambda q=q: (lambda e: e.dma_start(
                        out=BrepB[q][:, :],
                        in_=xpB_h[4 * q:4 * q + 4, :].unsqueeze(0).broadcast_to([32, 4, TC]))))(),
                        waits=[("xst", xpB_st)], inc="bc", dma=True)
                    S.op("s", (lambda q=q: (lambda e: e.dma_start(
                        out=CrepB[q][:, :],
                        in_=xpB_h[16 + 4 * q:16 + 4 * q + 4, :].unsqueeze(0).broadcast_to([32, 4, TC]))))(),
                        inc="bc", dma=True)
                st["bc_done"] = S.v("bc")

            def p1_subB_head(c, blk):
                co = c * TC
                par = blk % 2
                w = [("qv", st["xp_evac"])]
                if (c, blk - 2) in dtps_free:
                    w.append(("qs", dtps_free[(c, blk - 2)]))
                elif c > 0:
                    w.append(("qs", dtps_free[(c - 1, BLKS - 2 + blk)]))
                S.op("t", (lambda b=blk, pr=par: (lambda e: e.matmul(
                    psdt[pr][:, 0:TC], dtw_sb[:, b * 128:(b + 1) * 128],
                    dtin[:, :], start=True, stop=True)))(),
                    waits=w, inc="T")
                sw = [("T", S.v("T"))]
                if (c, blk - 2) in dtb_cons:
                    r0, r1 = dtb_cons[(c, blk - 2)]
                    sw += [("rp0", r0), ("rp1", r1)]
                elif c > 0:
                    r0, r1 = dtb_cons[(c - 1, BLKS - 2 + blk)]
                    sw += [("rp0", r0), ("rp1", r1)]
                S.op("s", (lambda b=blk, pr=par: (lambda e: e.activation(
                    es[:, :], psdt[pr][:, 0:TC], AF.Exp,
                    bias=bcol[:, b:b + 1])))(),
                    waits=sw, inc=None)
                S.op("s", (lambda pr=par: (lambda e: e.activation(
                    dtb[pr][:, :], es[:, :], AF.Ln, bias=1.0)))(), inc="qs")
                dtps_free[(c, blk)] = S.v("qs")
                dt_done = S.v("qs")
                sxb = f"xb{par}"
                if (c, blk - 2) in xln_free:
                    w = [("qv", xln_free[(c, blk - 2)])]
                elif c > 0:
                    w = [("qv", xln_free[(c - 1, BLKS - 2 + blk)])]
                else:
                    w = []
                S.op("y", (lambda pr=par, b=blk, co=co: (lambda e: e.dma_start(
                    out=xs2[pr][:, :],
                    in_=x_dt[b * 128:(b + 1) * 128, co:co + TC])))(),
                    waits=w, inc=sxb, dma=True)
                S.op("v", (lambda pr=par, co=co: (lambda e: e.tensor_tensor(
                    tmpb[:, :], xs2[pr][:, :], rbc[:, co:co + TC], ALU.mult)))(),
                    waits=[(sxb, S.v(sxb))], inc=None)
                S.op("v", (lambda pr=par, co=co: (lambda e: e.tensor_tensor(
                    xln2[pr][:, :], tmpb[:, :], mrbc[:, co:co + TC], ALU.subtract)))(),
                    inc=None)
                S.op("v", (lambda pr=par: (lambda e: e.tensor_tensor(
                    ub[pr][:, :], dtb[pr][:, :], xln2[pr][:, :], ALU.mult)))(),
                    waits=[("qs", dt_done)], inc="qv")
                ud[(c, blk)] = S.v("qv")

            def p1_subB_gls(c, blk):
                par = blk % 2
                bc_done = st["bc_done"]
                u_done = ud[(c, blk)]
                for gl in range(4):
                    gp = gl % 2
                    rp = f"rp{gp}"
                    gslot = c * BLKS * 4 + blk * 4 + gl
                    w = [("qv", u_done), ("bc", bc_done)]
                    if gslot - 2 in rep_free:
                        rs, rv = rep_free[gslot - 2]
                        w += [("qs", rs), ("qv", rv)]
                    S.op("s", (lambda pr=par, g=gp, gl=gl: (lambda e: e.dma_start(
                        out=dtrep[g][:, :],
                        in_=dtb[pr].ap()[32 * gl:32 * gl + 32, :].unsqueeze(1).broadcast_to([32, 4, TC]))))(),
                        waits=w, inc=rp, dma=True)
                    S.op("s", (lambda pr=par, g=gp, gl=gl: (lambda e: e.dma_start(
                        out=urep[g][:, :],
                        in_=ub[pr].ap()[32 * gl:32 * gl + 32, :].unsqueeze(1).broadcast_to([32, 4, TC]))))(),
                        inc=rp, dma=True)
                    rep_rdy = S.v(rp)
                    dtb_cons[(c, blk)] = (S.v("rp0"), S.v("rp1"))
                    lastS = lastV = 0
                    for q in range(4):
                        tile = (blk * 4 + gl) * 4 + q
                        gtile = c * G4 + tile
                        bp = gtile % NDA
                        hsi = gtile % HSD
                        w = [(rp, rep_rdy)]
                        if ("dA", gtile - NDA) in cons:
                            w.append(("qv", cons[("dA", gtile - NDA)]))
                        S.op("s", (lambda g=gp, b=bp, col=tile: (lambda e: e.activation(
                            dAb[b][:, :], dtrep[g][:, :], AF.Exp,
                            scale=A_il[:, col:col + 1])))(),
                            waits=w, inc="qs")
                        lastS = S.v("qs")
                        if gtile - NDA in pend_stc:
                            t2, h2, v2 = pend_stc.pop(gtile - NDA)
                            S.op("s", (lambda t=t2, h=h2: (lambda e: e.copy(
                                states[:, t:t + 1], h_sb[h][:, TC - 1:TC])))(),
                                waits=[("qv", v2)], inc="qs")
                            lastS = S.v("qs")
                        w = [(rp, rep_rdy), ("bc", bc_done)]
                        S.op("v", (lambda g=gp, b=bp, q=q: (lambda e: e.tensor_tensor(
                            dBxb[b][:, :], urep[g][:, :], BrepB[q][:, :], ALU.mult)))(),
                            waits=w, inc="qv")
                        lastV = S.v("qv")
                        tw = [("qs", lastS)]
                        S.op("v", (lambda b=bp, h=hsi, t=tile: (lambda e: e.tensor_tensor_scan(
                            h_sb[h][:, :], dAb[b][:, :], dBxb[b][:, :],
                            states[:, t:t + 1], ALU.mult, ALU.add)))(),
                            waits=tw, inc="qv")
                        cons[("dA", gtile)] = S.v("qv")
                        if c < NP - 1:
                            pend_stc[gtile] = (tile, hsi, S.v("qv"))
                        bph = gtile % NHC
                        w = []
                        if ("hC", gtile - NHC) in cons:
                            w.append(("T", cons[("hC", gtile - NHC)]))
                        S.op("v", (lambda b=bph, h=hsi, q=q: (lambda e: e.tensor_tensor(
                            hCb[b][:, :], h_sb[h][:, :], CrepB[q][:, :], ALU.mult)))(),
                            waits=w, inc="qv")
                        vhc = S.v("qv")
                        w = [("qv", vhc)]
                        if q == 0 and gl == 0:
                            if (c, blk - 2) in yps_free:
                                w.append(("qv", yps_free[(c, blk - 2)]))
                            elif c > 0:
                                w.append(("qv", yps_free[(c - 1, BLKS - 2 + blk)]))
                        S.op("t", (lambda b=bph, gl=gl, q=q, pr=par: (lambda e: e.matmul(
                            psy[pr][32 * gl:32 * gl + 32, 0:TC], r4_sb[:, :],
                            hCb[b][:, :],
                            start=(q == 0), stop=(q == 3),
                            tile_position=(0, 32 * gl))))(),
                            waits=w, inc="T")
                        cons[("hC", gtile)] = S.v("T")
                    rep_free[gslot] = (lastS, lastV)
                    if gl == 0 and blk > 0:
                        p1_subB_y(c, blk - 1)
                    if gl == 1 and blk + 1 < BLKS:
                        p1_subB_head(c, blk + 1)
                    pump(2)
                ypstop[(c, blk)] = S.v("T")

            def p1_subB_y(c, blk):
                co = c * TC
                par = blk % 2
                syh = f"yh{par}"
                pyh = f"yh{(blk - 1) % 2}"
                w = [("T", ypstop[(c, blk)])]
                if (c, blk - 1) in ybl_free:
                    w.append((pyh, ybl_free[(c, blk - 1)]))
                elif c > 0:
                    w.append((pyh, ybl_free[(c - 1, BLKS - 1)]))
                S.op("v", (lambda pr=par, b=blk: (lambda e: e.scalar_tensor_tensor(
                    yb[pr][:, :], xln2[pr][:, :],
                    dcol[:, b:b + 1], psy[pr][:, 0:TC],
                    ALU.mult, ALU.add)))(),
                    waits=w, inc="qv")
                xln_free[(c, blk)] = S.v("qv")
                yps_free[(c, blk)] = S.v("qv")
                S.op("y", (lambda pr=par, b=blk, co=co: (lambda e: e.dma_start(
                    out=y_h[b * 128:(b + 1) * 128, co:co + TC],
                    in_=yb[pr][:, :])))(),
                    waits=[("qv", S.v("qv"))], inc=syh, dma=True)
                ybl_free[(c, blk)] = S.v(syh)

            def p2_stats_blk(pss, blk):
                pc = pss * Tf
                yA = pss * TC
                yB = pss * TC + Tf
                hc = (pss % 2) * Tf
                par = blk % 3
                sxy, sxs = f"xy{par}", f"xs{par}"
                w = [(sxs, x1_done[par])] if par in x1_done else []
                w.append((f"yh{blk % 2}", ybl_free[(pss, blk)]))
                S.op("y", (lambda p=par, b=blk, pc=pc: (lambda e: e.dma_start(
                    out=xa[p][:, :], in_=xcols[b * 128:(b + 1) * 128, pc:pc + Tf])))(),
                    waits=w, inc=sxy, dma=True)
                S.op("y", (lambda p=par, b=blk, yA=yA: (lambda e: e.dma_start(
                    out=yba[p][:, :], in_=y_h[b * 128:(b + 1) * 128, yA:yA + Tf])))(),
                    inc=sxy, dma=True)
                S.op("y", (lambda p=par, b=blk, yB=yB: (lambda e: e.dma_start(
                    out=ybb[p][:, :], in_=y_h[b * 128:(b + 1) * 128, yB:yB + Tf])))(),
                    inc=sxy, dma=True)
                stw = [(sxy, S.v(sxy))]
                if par in x1c:
                    stw.append(("qs", x1c[par]))
                S.op("v", (lambda p=par: (lambda e: e.scalar_tensor_tensor(
                    ftst[:, :], yba[p][:, :], mA_sb[:, 0:1], xa[p][:, :],
                    ALU.mult, ALU.add)))(),
                    waits=stw, inc=None)
                S.op("v", (lambda p=par: (lambda e: e.scalar_tensor_tensor(
                    x1b[p][:, :], ybb[p][:, :], mB_sb[:, 0:1], ftst[:, :],
                    ALU.mult, ALU.add)))(), inc="qv")
                S.op("y", (lambda p=par, b=blk, pc=pc: (lambda e: e.dma_start(
                    out=x1_h[b * 128:(b + 1) * 128, pc:pc + Tf], in_=x1b[p][:, :])))(),
                    waits=[("qv", S.v("qv"))], inc=sxs, dma=True)
                x1_done[par] = S.v(sxs)
                w = [("qv", S.v("qv"))]
                if par in stat_done:
                    w.append(("T", stat_done[par]))
                S.op("s", (lambda p=par: (lambda e: e.copy(
                    sbf[p][:, :], x1b[p][:, :])))(), waits=w, inc=None)
                S.op("s", (lambda p=par: (lambda e: e.square(
                    ssq[p][:, :], sbf[p][:, :])))(), inc="qs")
                x1c[par] = S.v("qs")

            def p2_stats_mm(pss, blk):
                hc = (pss % 2) * Tf
                par = blk % 3
                sw = [("qs", x1c[par])]
                if blk == 0 and pss >= 1:
                    sw.append(("qs", ln2_read[pss - 1]))
                S.op("t", (lambda p=par, b=blk, hc=hc: (lambda e: e.matmul(
                    ps[7][0:1, hc:hc + Tf], onesbf[:, :], sbf[p][:, :],
                    start=(b == 0), stop=(b == BLKS - 1),
                    tile_position=(0, 0))))(),
                    waits=sw, inc=None)
                S.op("t", (lambda p=par, b=blk, hc=hc: (lambda e: e.matmul(
                    ps[7][32:33, hc:hc + Tf], onesbf[:, :], ssq[p][:, :],
                    start=(b == 0), stop=(b == BLKS - 1),
                    tile_position=(0, 32))))(), inc="T")
                stat_done[par] = S.v("T")

            def rows_chain(src_m, src_q, waits):
                S.op("s", (lambda sm=src_m: (lambda e: e.activation(
                    m2r[0:1, 0:Tf], sm, AF.Copy, scale=1.0 / D)))(),
                    waits=waits, inc=None)
                S.op("s", (lambda sq=src_q: (lambda e: e.activation(
                    e2r[0:1, 0:Tf], sq, AF.Copy, scale=1.0 / D)))(),
                    inc="qs")
                S.op("v", lambda e: e.tensor_tensor(
                    t2r[:, :], m2r[:, :], m2r[:, :], ALU.mult),
                    waits=[("qs", S.v("qs"))], inc=None)
                S.op("v", lambda e: e.tensor_tensor(
                    t2r[:, :], e2r[:, :], t2r[:, :], ALU.subtract), inc="qv")
                S.op("s", lambda e: e.activation(
                    r2r[:, :], t2r[:, :], AF.Ln, bias=epscol[0:1, 0:1]),
                    waits=[("qv", S.v("qv"))], inc=None)
                S.op("s", lambda e: e.activation(
                    r2r[:, :], r2r[:, :], AF.Exp, scale=-0.5), inc="qs")
                S.op("v", lambda e: e.tensor_tensor(
                    t2r[:, :], m2r[:, :], r2r[:, :], ALU.mult),
                    waits=[("qs", S.v("qs"))], inc="qv")

            def bc_pair():
                S.op("t", lambda e: e.matmul(
                    ps[6][:, 0:Tf], onesf[:, :], r2r[0:1, 0:Tf], start=True, stop=True),
                    waits=[("qv", S.v("qv"))], inc="T")
                S.op("v", lambda e: e.tensor_copy(bca[:, :], ps[6][:, 0:Tf]),
                     waits=[("T", S.v("T"))], inc="qv")
                S.op("t", lambda e: e.matmul(
                    ps[6][:, 0:Tf], onesf[:, :], t2r[0:1, 0:Tf], start=True, stop=True),
                    waits=[("qv", S.v("qv"))], inc="T")
                S.op("v", lambda e: e.tensor_copy(bcb[:, :], ps[6][:, 0:Tf]),
                     waits=[("T", S.v("T"))], inc="qv")

            def ws_gen(wd, Krows, OD, rhs_of_k, evac, extra_w=()):
                for mg, (mgo, mgn) in enumerate(nsplit(OD, MGW)):
                    nml = (mgn + 127) // 128
                    assert mgn % 128 == 0 and nml <= 3
                    rch = nsplit(Krows, KROW)
                    for ri, (ro, rn) in enumerate(rch):
                        wi = wbi[0] % 2
                        wbi[0] += 1
                        sld = f"wl{wi}"
                        w = list(extra_w) + [("ld0", st["precast"])]
                        if wi in wb_free:
                            w.append(("T", wb_free[wi]))
                        S.op("y", (lambda wd=wd, ro=ro, rn=rn, mgo=mgo, mgn=mgn, wi=wi: (lambda e: e.dma_start(
                            out=wb[wi][:, 0:(rn // 128) * mgn],
                            in_=wd[ro:ro + rn, mgo:mgo + mgn].rearrange("(k p) c -> p k c", p=128))))(),
                            waits=w, inc=sld, dma=True)
                        lw = [(sld, S.v(sld))]
                        if ri == 0 and mg >= 1:
                            lw.append(("qs", ws_ps_free.get("S", 0)))
                            lw.append(("qv", ws_ps_free.get("V", 0)))
                        for k in range(rn // 128):
                            for ml in range(nml):
                                S.op("t", (lambda wi=wi, k=k, ml=ml, mgn=mgn, ro=ro,
                                           pt=ps[4 + ml], rhs=rhs_of_k,
                                           first=(ri == 0 and k == 0),
                                           last=(ri == len(rch) - 1 and k == rn // 128 - 1): (lambda e: e.matmul(
                                    pt[0:128, 0:Tf],
                                    wb[wi][:, k * mgn + ml * 128:k * mgn + ml * 128 + 128],
                                    rhs(ro // 128 + k), start=first, stop=last)))(),
                                    waits=lw if (k == 0 and ml == 0) else (),
                                    inc="T" if (k == rn // 128 - 1 and ml == nml - 1) else None)
                        wb_free[wi] = S.v("T")
                        yield
                    evac(mg, mgo, mgn, nml, [("T", S.v("T"))])
                    yield

            def p2_pass_gen(pss, x1d):
                pc = pss * Tf
                hc = (pss % 2) * Tf
                rows_chain(ps[7][0:1, hc:hc + Tf], ps[7][32:33, hc:hc + Tf],
                           [("T", S.v("T"))])
                ln2_read[pss] = S.v("qs")
                for _ in range(5):
                    yield
                bc_pair()
                yield
                xe = {}
                for blk in range(BLKS):
                    par = blk % 2
                    sxr = f"xr{par}"
                    w = ([("qv", S.v("qv"))] +
                         [(f"xs{j}", x1d[j]) for j in sorted(x1d)]
                         if blk < 2 else [("qv", xe[blk - 2])])
                    S.op("y", (lambda p=par, b=blk, pc=pc: (lambda e: e.dma_start(
                        out=x1r[p][:, :], in_=x1_h[b * 128:(b + 1) * 128, pc:pc + Tf])))(),
                        waits=w, inc=sxr, dma=True)
                    S.op("v", (lambda p=par: (lambda e: e.tensor_tensor(
                        ft1[:, :], x1r[p][:, :], bca[:, :], ALU.mult)))(),
                        waits=[(sxr, S.v(sxr))], inc=None)
                    S.op("v", (lambda b=blk: (lambda e: e.tensor_tensor(
                        xlnff[:, b * Tf:(b + 1) * Tf], ft1[:, :], bcb[:, :], ALU.subtract)))(),
                        inc="qv")
                    xe[blk] = S.v("qv")
                    if blk % 8 == 7:
                        yield
                xln_done = S.v("qv")
                for _ in range(6):
                    yield
                gq = {}
                for mi, (mo, mn) in enumerate(MCH):
                    w = [("qv", xln_done)]
                    if mi - 1 in gq:
                        w.append(("qv", gq[mi - 1]))
                    for k in range(BLKS):
                        S.op("t", (lambda k=k, mo=mo, mn=mn, mi=mi: (lambda e: e.matmul(
                            ps[6][0:mn, mi * 8:mi * 8 + NE],
                            xlnff[:, k * Tf + mo:k * Tf + mo + mn],
                            gate_sb[:, k * NE:(k + 1) * NE],
                            start=(k == 0), stop=(k == BLKS - 1))))(),
                            waits=w if k == 0 else (),
                            inc="T" if k == BLKS - 1 else None)
                    S.op("v", (lambda mi=mi, mn=mn: (lambda e: e.max(
                        mx8[0:mn, :], ps[6][0:mn, mi * 8:mi * 8 + NE])))(),
                        waits=[("T", S.v("T"))], inc=None)
                    S.op("v", (lambda mi=mi, mn=mn: (lambda e: e.tensor_tensor(
                        rwcols[0:mn, mi:mi + 1], mx8[0:mn, 0:1], mx8[0:mn, 1:2], ALU.add)))(),
                        inc="qv")
                    gq[mi] = S.v("qv")
                for _ in range(4):
                    yield
                for mi, (mo, mn) in enumerate(MCH):
                    S.op("t", (lambda mi=mi, mn=mn: (lambda e: e.transpose(
                        ps[6][0:1, 0:mn], rwcols[0:mn, mi:mi + 1], ident[0:mn, 0:mn])))(),
                        waits=[("qv", S.v("qv"))], inc="T")
                    S.op("v", (lambda mn=mn: (lambda e: e.tensor_copy(
                        rwr1[0:1, 0:mn], ps[6][0:1, 0:mn])))(),
                        waits=[("T", S.v("T"))], inc="qv")
                    S.op("t", (lambda mi=mi, mo=mo, mn=mn: (lambda e: e.matmul(
                        ps[6][:, mo:mo + mn], onesf[:, :], rwr1[0:1, 0:mn],
                        start=True, stop=True)))(),
                        waits=[("qv", S.v("qv"))], inc="T")
                    S.op("v", (lambda mo=mo, mn=mn: (lambda e: e.tensor_copy(
                        rwbc[:, mo:mo + mn], ps[6][:, mo:mo + mn])))(),
                        waits=[("T", S.v("T"))], inc="qv")
                rw_done = S.v("qv")
                yield

                def evac_fc1(mg, mgo, mgn, nml, tw):
                    for ml in range(nml):
                        sl = mgo // 128 + ml
                        S.op("s", (lambda sl=sl, pt=ps[4 + ml]: (lambda e: e.copy(
                            lat_sb[:, sl * Tf:(sl + 1) * Tf],
                            pt[0:128, 0:Tf])))(),
                            waits=tw if ml == 0 else (),
                            inc="qs" if ml == nml - 1 else None)
                    ws_ps_free["S"] = S.v("qs")
                    ws_ps_free["V"] = S.v("qv")

                yield from ws_gen(fc1b, D, LAT, lambda k: xlnff[:, k * Tf:(k + 1) * Tf],
                                  evac_fc1, extra_w=[("qv", rw_done)])
                lat_done = S.v("qs")

                def evac_exp(mg, mgo, mgn, nml, tw):
                    for ml in range(nml):
                        sl = mgo // 128 + ml
                        S.op("v", (lambda sl=sl, pt=ps[4 + ml]: (lambda e: e.tensor_tensor(
                            moe_sb[:, sl * Tf:(sl + 1) * Tf],
                            pt[0:128, 0:Tf], rwbc[:, :], ALU.mult)))(),
                            waits=tw if ml == 0 else (),
                            inc="qv" if ml == nml - 1 else None)
                    ws_ps_free["S"] = S.v("qs")
                    ws_ps_free["V"] = S.v("qv")

                yield from ws_gen(expb, LAT, LAT, lambda k: lat_sb[:, k * Tf:(k + 1) * Tf],
                                  evac_exp, extra_w=[("qs", lat_done), ("qv", rw_done)])
                moe_done = S.v("qv")

                def evac_up(mg, mgo, mgn, nml, tw):
                    ufx = upst[0]
                    upst[0] += 1
                    hb = ufx % 2
                    shs = f"hs{hb}"
                    w = list(tw)
                    if ufx - 2 in hio_free:
                        w.append((shs, hio_free[ufx - 2]))
                    for ml in range(nml):
                        fi = ftrot[0] % 2
                        ftrot[0] += 1
                        fa = fu[fi]
                        sw = list(w) if ml == 0 else []
                        if fi in fu_free:
                            sw.append(("qv", fu_free[fi]))
                        S.op("s", (lambda fa=fa, pt=ps[4 + ml]: (lambda e: e.activation(
                            fa[:, :], pt[0:128, 0:Tf], AF.Relu)))(),
                            waits=sw, inc="qs")
                        S.op("v", (lambda fa=fa, ml=ml, hb=hb, pt=ps[4 + ml]: (lambda e: e.tensor_tensor(
                            hio[hb][:, ml * Tf:(ml + 1) * Tf], fa[:, :],
                            pt[0:128, 0:Tf], ALU.mult)))(),
                            waits=[("qs", S.v("qs"))], inc="qv")
                        fu_free[fi] = S.v("qv")
                    ws_ps_free["S"] = S.v("qs")
                    ws_ps_free["V"] = S.v("qv")
                    stw = [("qv", S.v("qv"))]
                    if pss > 0:
                        stw += [("hl0", h_rd[pss - 1][0]), ("hl1", h_rd[pss - 1][1])]
                    S.op("y", (lambda mgo=mgo, mgn=mgn, hb=hb, nml=nml: (lambda e: e.dma_start(
                        out=h_h[mgo:mgo + mgn, :].rearrange(
                            "(m p) t -> p m t", p=128),
                        in_=hio[hb][:, 0:nml * Tf])))(),
                        waits=stw, inc=shs, dma=True)
                    hio_free[ufx] = S.v(shs)

                yield from ws_gen(upb, D, INT, lambda k: xlnff[:, k * Tf:(k + 1) * Tf],
                                  evac_up, extra_w=[("qv", moe_done)])
                up_done = [("hs0", S.v("hs0")), ("hs1", S.v("hs1"))]

                sdone = {}
                hl_free = {}
                for mg, (mgo, mgn) in enumerate(nsplit(D, MGW)):
                    nml = (mgn + 127) // 128
                    rch = nsplit(INT, KROW)
                    for ri, (ro, rn) in enumerate(rch):
                        wi = wbi[0] % 2
                        wbi[0] += 1
                        sld = f"wl{wi}"
                        fidx = mg * len(rch) + ri
                        hb = fidx % 2
                        shl = f"hl{hb}"
                        w = list(up_done) + [("ld0", st["precast"])]
                        if wi in wb_free:
                            w.append(("T", wb_free[wi]))
                        S.op("y", (lambda ro=ro, rn=rn, mgo=mgo, mgn=mgn, wi=wi: (lambda e: e.dma_start(
                            out=wb[wi][:, 0:(rn // 128) * mgn],
                            in_=downb[ro:ro + rn, mgo:mgo + mgn].rearrange(
                                "(k p) c -> p k c", p=128))))(),
                            waits=w, inc=sld, dma=True)
                        hw = list(up_done)
                        if fidx - 2 in hl_free:
                            hw.append(("T", hl_free[fidx - 2]))
                        S.op("y", (lambda ro=ro, rn=rn, hb=hb: (lambda e: e.dma_start(
                            out=hio[hb][:, 0:(rn // 128) * Tf],
                            in_=h_h[ro:ro + rn, :].rearrange("(k p) t -> p k t", p=128))))(),
                            waits=hw, inc=shl, dma=True)
                        lw = [(sld, S.v(sld)), (shl, S.v(shl))]
                        if ri == 0 and mg >= 1:
                            lw.append(("qs", sdone[mg - 1]))
                        for k in range(rn // 128):
                            for ml in range(nml):
                                S.op("t", (lambda wi=wi, hb=hb, k=k, ml=ml, mgn=mgn,
                                            pt=ps[4 + ml],
                                            first=(ri == 0 and k == 0): (lambda e: e.matmul(
                                    pt[0:128, 0:Tf],
                                    wb[wi][:, k * mgn + ml * 128:k * mgn + ml * 128 + 128],
                                    hio[hb][:, k * Tf:(k + 1) * Tf],
                                    start=first, stop=False)))(),
                                    waits=lw if (k == 0 and ml == 0) else (),
                                    inc="T" if (k == rn // 128 - 1 and ml == nml - 1) else None)
                        wb_free[wi] = S.v("T")
                        hl_free[fidx] = S.v("T")
                        yield
                    rch2 = nsplit(LAT, KROW)
                    for ri2, (ro2, rn2) in enumerate(rch2):
                        wi = wbi[0] % 2
                        wbi[0] += 1
                        sld = f"wl{wi}"
                        w = [("qv", moe_done), ("ld0", st["precast"])]
                        if wi in wb_free:
                            w.append(("T", wb_free[wi]))
                        S.op("y", (lambda ro2=ro2, rn2=rn2, mgo=mgo, mgn=mgn, wi=wi: (lambda e: e.dma_start(
                            out=wb[wi][:, 0:(rn2 // 128) * mgn],
                            in_=fc2b[ro2:ro2 + rn2, mgo:mgo + mgn].rearrange("(k p) c -> p k c", p=128))))(),
                            waits=w, inc=sld, dma=True)
                        lw = [(sld, S.v(sld))]
                        for k in range(rn2 // 128):
                            for ml in range(nml):
                                S.op("t", (lambda wi=wi, k=k, kk=ro2 // 128 + k, ml=ml, mgn=mgn,
                                            pt=ps[4 + ml],
                                            last=(ri2 == len(rch2) - 1 and k == rn2 // 128 - 1): (lambda e: e.matmul(
                                    pt[0:128, 0:Tf],
                                    wb[wi][:, k * mgn + ml * 128:k * mgn + ml * 128 + 128],
                                    moe_sb[:, kk * Tf:(kk + 1) * Tf],
                                    start=False, stop=last)))(),
                                    waits=lw if (k == 0 and ml == 0) else (),
                                    inc="T" if (k == rn2 // 128 - 1 and ml == nml - 1) else None)
                        wb_free[wi] = S.v("T")
                        yield
                    tw = [("T", S.v("T"))]
                    for ml in range(nml):
                        sl = mgo // 128 + ml
                        S.op("s", (lambda sl=sl, pt=ps[4 + ml]: (lambda e: e.activation(
                            s_sb[:, sl * Tf:(sl + 1) * Tf],
                            pt[0:128, 0:Tf], AF.Copy)))(),
                            waits=tw if ml == 0 else (),
                            inc="qs" if ml == nml - 1 else None)
                    sdone[mg] = S.v("qs")
                    ws_ps_free["S"] = S.v("qs")
                    ws_ps_free["V"] = S.v("qv")
                    yield
                h_rd[pss] = (S.v("hl0"), S.v("hl1"))
                s_done = S.v("qs")

                st3 = {}
                c3 = {}

                def ln3_mm(blk):
                    par = blk % 2
                    sw = [("qs", c3[blk])]
                    S.op("t", (lambda p=par, b=blk, hc=hc: (lambda e: e.matmul(
                        ps[7][64:65, hc:hc + Tf], onesbf[:, :], sbf3[p][:, :],
                        start=(b == 0), stop=(b == BLKS - 1),
                        tile_position=(0, 64))))(),
                        waits=sw, inc=None)
                    S.op("t", (lambda p=par, b=blk, hc=hc: (lambda e: e.matmul(
                        ps[7][96:97, hc:hc + Tf], onesbf[:, :], ssq3[p][:, :],
                        start=(b == 0), stop=(b == BLKS - 1),
                        tile_position=(0, 96))))(), inc="T")
                    st3[blk] = S.v("T")

                for blk in range(BLKS):
                    par = blk % 2
                    w = [("qs", s_done)]
                    if blk - 2 in st3:
                        w.append(("T", st3[blk - 2]))
                    S.op("s", (lambda p=par, b=blk: (lambda e: e.copy(
                        sbf3[p][:, :], s_sb[:, b * Tf:(b + 1) * Tf])))(),
                        waits=w, inc=None)
                    S.op("s", (lambda p=par: (lambda e: e.square(
                        ssq3[p][:, :], sbf3[p][:, :])))(), inc="qs")
                    c3[blk] = S.v("qs")
                    if blk >= 1:
                        ln3_mm(blk - 1)
                    if blk % 4 == 3:
                        yield
                ln3_mm(BLKS - 1)
                yield
                rows_chain(ps[7][64:65, hc:hc + Tf], ps[7][96:97, hc:hc + Tf],
                           [("T", S.v("T"))])
                for _ in range(5):
                    yield
                bc_pair()
                yield
                for blk in range(BLKS):
                    par = blk % 2
                    sxr, sot = f"xr{par}", f"ot{par}"
                    w = [("qv", S.v("qv"))]
                    if (pss, blk - 2) in odone:
                        w.append((sot, odone[(pss, blk - 2)]))
                    elif pss > 0:
                        w.append((sot, odone[(pss - 1, BLKS - 2 + par)]))
                    S.op("y", (lambda p=par, b=blk, pc=pc: (lambda e: e.dma_start(
                        out=x1r[p][:, :], in_=x1_h[b * 128:(b + 1) * 128, pc:pc + Tf])))(),
                        waits=w, inc=sxr, dma=True)
                    vw = [(sxr, S.v(sxr))]
                    if (pss, blk - 2) in odone:
                        vw.append((sot, odone[(pss, blk - 2)]))
                    elif pss > 0:
                        vw.append((sot, odone[(pss - 1, BLKS - 2 + par)]))
                    S.op("v", (lambda b=blk: (lambda e: e.tensor_tensor(
                        ft1[:, :], s_sb[:, b * Tf:(b + 1) * Tf], bca[:, :], ALU.mult)))(),
                        waits=vw, inc=None)
                    S.op("v", lambda e: e.tensor_tensor(
                        ft2[:, :], ft1[:, :], bcb[:, :], ALU.subtract), inc=None)
                    S.op("v", (lambda b=blk: (lambda e: e.tensor_scalar(
                        ft1[:, :], ft2[:, :], gam_sb[:, b:b + 1], bet_sb[:, b:b + 1],
                        ALU.mult, ALU.add)))(), inc=None)
                    S.op("v", (lambda p=par: (lambda e: e.tensor_tensor(
                        ob[p][:, :], ft1[:, :], x1r[p][:, :], ALU.add)))(), inc="qv")
                    S.op("y", (lambda p=par, b=blk, pc=pc: (lambda e: e.dma_start(
                        out=out_dt[b * 128:(b + 1) * 128, pc:pc + Tf], in_=ob[p][:, :])))(),
                        waits=[("qv", S.v("qv"))], inc=sot, dma=True)
                    odone[(pss, blk)] = S.v(sot)
                    if blk % 8 == 7:
                        yield

            for c in range(NP):
                p1_subA(c)
                if c == 0:
                    sched_precast()
                p1_subB_head(c, 0)
                for blk in range(BLKS):
                    p1_subB_gls(c, blk)
                    if blk >= 1:
                        p2_stats_blk(c, blk - 1)
                        if blk >= 2:
                            if blk == 2:
                                while c >= 1 and (c - 1) not in ln2_read:
                                    pump(5)
                            p2_stats_mm(c, blk - 2)
                    pump(1)
                p1_subB_y(c, BLKS - 1)
                p2_stats_blk(c, BLKS - 1)
                p2_stats_mm(c, BLKS - 2)
                p2_stats_mm(c, BLKS - 1)
                pending.append(p2_pass_gen(c, dict(x1_done)))
            while pending:
                pump(100)

            S.op("y", lambda e: e.nop(),
                 waits=[("ot0", S.v("ot0")), ("ot1", S.v("ot1"))])
            S.emit()
    return nc



def _stage_core(cfg, inputs, b, half):
    D, T = cfg.D, cfg.T
    BLKS, TC, Tf, G4 = cfg.BLKS, cfg.TC, cfg.Tf, cfg.G4
    f32 = np.float32
    x = np.asarray(inputs["L0_attn_x"][b], f32)
    x_dt = np.ascontiguousarray(x.T)
    tok = np.concatenate([np.arange(p * TC + half * Tf, p * TC + half * Tf + Tf)
                          for p in range(cfg.NPASS)])
    xcols = np.ascontiguousarray(x_dt[:, tok])

    A_log = np.asarray(inputs["L0_attn_A_log"], f32)
    Ar = A_log.reshape(D // 32, 32, 4, 4)
    alog_il = np.ascontiguousarray(
        Ar.transpose(1, 3, 0, 2).reshape(128, G4))

    cm = lambda v: np.ascontiguousarray(np.asarray(v, f32).reshape(BLKS, 128).T)
    r4 = np.zeros((128, 32), f32)
    for p in range(128):
        r4[p, p // 4] = 1.0

    z = np.zeros((128, 1), f32)
    return {
        "x_dt": x_dt,
        "xcols": xcols,
        "up_w": np.ascontiguousarray(inputs["L0_ffn_up_w"], f32),
        "down_w": np.ascontiguousarray(inputs["L0_ffn_down_w"], f32),
        "fc1_w": np.ascontiguousarray(inputs["L0_ffn_fc1_w"], f32),
        "fc2_w": np.ascontiguousarray(inputs["L0_ffn_fc2_w"], f32),
        "expert_w": np.ascontiguousarray(inputs["L0_ffn_expert_w"], f32),
        "gate_w": np.ascontiguousarray(inputs["L0_ffn_gate_w"], f32),
        "xpw_t": np.ascontiguousarray(np.asarray(inputs["L0_attn_x_proj_w"], f32).T),
        "dtw_t": np.ascontiguousarray(np.asarray(inputs["L0_attn_dt_proj_w"], f32).T),
        "alog_il": alog_il,
        "b_cm": cm(inputs["L0_attn_dt_proj_b"]),
        "d_cm": cm(inputs["L0_attn_D"]),
        "gam_cm": cm(inputs["L0_ffn_ln_gamma"]),
        "bet_cm": cm(inputs["L0_ffn_ln_beta"]),
        "r4": r4.astype(ml_dtypes.bfloat16),
        "ones_bf": np.ones((128, 1), ml_dtypes.bfloat16),
        "ones_f": np.ones((1, 128), f32),
        "ident_f": np.eye(128, dtype=f32),
        "mask_a": z + (0.0 if half else 1.0),
        "mask_b": z + (1.0 if half else 0.0),
    }, tok


def kernel(**inputs):
    cfg = CFG
    nc = build_core_program(cfg)
    in_maps, toks = [], []
    for k in range(8):
        im, tok = _stage_core(cfg, inputs, k // 2, k % 2)
        in_maps.append(im)
        toks.append(tok)
    res = run_bass_kernel_spmd(nc, in_maps, core_ids=list(range(8)))
    B = inputs["L0_attn_x"].shape[0]
    out = np.empty((B, cfg.T, cfg.D), np.float32)
    for k in range(8):
        out[k // 2, toks[k], :] = np.asarray(res.results[k]["out"]).T
    return out


# version 27
# speedup vs baseline: 1.2787x; 1.1281x over previous
from contextlib import ExitStack
from dataclasses import dataclass

import numpy as np
import ml_dtypes
import concourse.bass as bass
import concourse.mybir as mybir
from concourse.bass_utils import run_bass_kernel_spmd

F32 = mybir.dt.float32
BF16 = mybir.dt.bfloat16
AF = mybir.ActivationFunctionType
ALU = mybir.AluOpType


@dataclass(frozen=True)
class Cfg:
    D: int = 4096
    T: int = 2048
    R: int = 128
    INT: int = 8192
    LAT: int = 2048
    NE: int = 8
    NPASS: int = 2
    eps: float = 1e-5

    @property
    def BLKS(self):
        return self.D // 128

    @property
    def TC(self):
        return self.T // self.NPASS

    @property
    def Tf(self):
        return self.TC // 2

    @property
    def TOKS(self):
        return self.T // 2

    @property
    def G4(self):
        return (self.D // 32) * 4


CFG = Cfg()
SEMS = ("ld0", "ld1", "xa0", "xa1", "xb0", "xb1", "rp0", "rp1", "bc", "xst",
        "yh0", "yh1", "xy0", "xy1", "xs0", "xs1", "xr0", "xr1",
        "hs0", "hs1", "hl0", "hl1", "ot0", "ot1", "T",
        "wl0", "wl1", "xy2", "xy3", "xs2", "xs3", "qv", "qs", "qg")
AUTOSYNC = {"v": "qv", "s": "qs", "g": "qg"}
KROW = 2048
DEBUG_DUMP = False
STRICT_SYNC = True


def nsplit(total, step=512):
    return [(i, min(step, total - i)) for i in range(0, total, step)]


class Sched:
    def __init__(self, nc, sems, cnt):
        self.nc, self.sems, self.cnt = nc, sems, cnt
        self.prog = []

    def op(self, eng, fn, waits=(), inc=None, dma=False):
        incs = []
        qn = AUTOSYNC.get(eng)
        if qn is not None and not dma:
            assert inc is None or inc == qn, (eng, inc)
            self.cnt[qn] += 1
            incs = [(qn, 1)]
        elif inc is not None:
            n = 16 if dma else 1
            self.cnt[inc] += n
            incs = [(inc, n)]
        self.prog.append((eng, tuple(waits), fn, incs, dma))

    def v(self, s):
        return self.cnt[s]

    def snap(self):
        return [(s, self.cnt[s]) for s in SEMS if self.cnt[s] > 0]

    def barrier(self):
        w = self.snap()
        for e in ("g", "v", "s", "t", "y"):
            self.prog.append((e, tuple(w), lambda en: en.nop(), [], True))

    def emit(self):
        with self.nc.Block() as block:
            def mk(key):
                def emitter(eng):
                    last = {}
                    qs = AUTOSYNC.get(key)
                    for (e, waits, fn, incs, is_dma) in self.prog:
                        if e != key:
                            continue
                        for (s, val) in waits:
                            if s == qs and not is_dma and not STRICT_SYNC:
                                continue
                            if val > 0 and last.get(s, 0) < val:
                                eng.wait_ge(self.sems[s], val)
                                last[s] = val
                        if STRICT_SYNC and qs is not None and not is_dma:
                            nq = self.cnt.get("em_" + qs, 0)
                            if nq > 0 and last.get(qs, 0) < nq:
                                eng.wait_ge(self.sems[qs], nq)
                                last[qs] = nq
                        instr = fn(eng)
                        for (s, n) in incs:
                            instr.then_inc(self.sems[s], n)
                            if s == qs:
                                self.cnt["em_" + qs] = self.cnt.get("em_" + qs, 0) + 1
                return emitter
            block.gpsimd(mk("g"))
            block.vector(mk("v"))
            block.scalar(mk("s"))
            block.tensor(mk("t"))
            block.sync(mk("y"))


def build_core_program(cfg: Cfg):
    nc = bass.Bass("TRN2", target_bir_lowering=False, debug=False)
    D, T, R, INT, LAT, NE = cfg.D, cfg.T, cfg.R, cfg.INT, cfg.LAT, cfg.NE
    BLKS, TC, Tf, TOKS, NP, G4 = cfg.BLKS, cfg.TC, cfg.Tf, cfg.TOKS, cfg.NPASS, cfg.G4
    RW = R + 32

    def dr(name, sh, dt=F32, kind=None):
        t = nc.dram_tensor(name, list(sh), dt, kind=kind) if kind else \
            nc.dram_tensor(name, list(sh), dt)
        return t.ap()

    x_dt = dr("x_dt", (D, T), kind="ExternalInput")
    xcols = dr("xcols", (D, TOKS), kind="ExternalInput")
    up_w = dr("up_w", (D, INT), kind="ExternalInput")
    down_w = dr("down_w", (INT, D), kind="ExternalInput")
    fc1_w = dr("fc1_w", (D, LAT), kind="ExternalInput")
    fc2_w = dr("fc2_w", (LAT, D), kind="ExternalInput")
    expert_w = dr("expert_w", (LAT, LAT), kind="ExternalInput")
    gate_w = dr("gate_w", (D, NE), kind="ExternalInput")
    xpw_t = dr("xpw_t", (D, RW), kind="ExternalInput")
    dtw_t = dr("dtw_t", (R, D), kind="ExternalInput")
    alog_il = dr("alog_il", (128, G4), kind="ExternalInput")
    b_cm = dr("b_cm", (128, BLKS), kind="ExternalInput")
    d_cm = dr("d_cm", (128, BLKS), kind="ExternalInput")
    gam_cm = dr("gam_cm", (128, BLKS), kind="ExternalInput")
    bet_cm = dr("bet_cm", (128, BLKS), kind="ExternalInput")
    r4_in = dr("r4", (128, 32), BF16, kind="ExternalInput")
    onesbf_in = dr("ones_bf", (128, 1), BF16, kind="ExternalInput")
    onesf_in = dr("ones_f", (1, 128), kind="ExternalInput")
    ident_in = dr("ident_f", (128, 128), kind="ExternalInput")
    mA_in = dr("mask_a", (128, 1), kind="ExternalInput")
    mB_in = dr("mask_b", (128, 1), kind="ExternalInput")
    out_dt = dr("out", (D, TOKS), kind="ExternalOutput")
    y_h = dr("y_h", (D, T))
    upb = dr("upb", (D, INT), BF16)
    downb = dr("downb", (INT, D), BF16)
    fc1b = dr("fc1b", (D, LAT), BF16)
    fc2b = dr("fc2b", (LAT, D), BF16)
    expb = dr("expb", (LAT, LAT), BF16)
    x1_h = dr("x1_h", (D, TOKS))
    h_h = dr("h_h", (INT, Tf), BF16)
    xpB_h = dr("xpB_h", (32, TC), BF16)
    dbg_h = dr("dbg_h", (128, 8 * TC))

    with ExitStack() as big:
        sb = lambda n, sh, dt=F32: big.enter_context(nc.sbuf_tensor(n, list(sh), dt))
        sems, cnt = {}, {}
        for s in SEMS:
            sems[s] = big.enter_context(nc.semaphore(s))
            cnt[s] = 0
        ps = [big.enter_context(nc.psum_tensor(f"ps{i}", [128, 512], F32))
              for i in range(8)]

        gate_sb = sb("gate_sb", (128, BLKS * NE), BF16)
        gam_sb = sb("gam_sb", (128, BLKS))
        bet_sb = sb("bet_sb", (128, BLKS))
        r4_sb = sb("r4_sb", (128, 32), BF16)
        onesbf = sb("onesbf", (128, 1), BF16)
        onesf = sb("onesf", (1, 128))
        ident = sb("ident", (128, 128))
        mA_sb = sb("mA_sb", (128, 1))
        mB_sb = sb("mB_sb", (128, 1))
        epscol = sb("epscol", (128, 1))

        with ExitStack() as mid:
            msb = lambda n, sh, dt=F32: mid.enter_context(nc.sbuf_tensor(n, list(sh), dt))
            xpw_sb = msb("xpw_sb", (128, BLKS * RW), BF16)
            dtw_sb = msb("dtw_sb", (R, D), BF16)
            A_il = msb("A_il", (128, G4))
            bcol = msb("bcol", (128, BLKS))
            dcol = msb("dcol", (128, BLKS))
            rbc = msb("rbc", (128, T))
            mrbc = msb("mrbc", (128, T))
            states = msb("states", (128, G4))
            rows4 = msb("rows4", (1, 4 * T))
            mrow = rows4.ap()[:, 0:T]
            e2row = rows4.ap()[:, T:2 * T]
            rrow = rows4.ap()[:, 2 * T:3 * T]
            trow = rows4.ap()[:, 3 * T:4 * T]

            S0 = Sched(nc, sems, cnt)
            with ExitStack() as p0:
                p0sb = lambda n, sh, dt=F32: p0.enter_context(
                    nc.sbuf_tensor(n, list(sh), dt))
                xst = [p0sb(f"xst{i}", (128, T)) for i in range(2)]
                xbf = [p0sb(f"xbf{i}", (128, T), BF16) for i in range(2)]
                xsq = [p0sb(f"xsq{i}", (128, T), BF16) for i in range(2)]
                atmp = p0sb("atmp", (128, G4))

                for dst, src in [
                    (xpw_sb, xpw_t.rearrange("(b p) r -> p b r", p=128)),
                    (dtw_sb, dtw_t),
                    (gate_sb, gate_w.rearrange("(b p) e -> p b e", p=128)),
                    (A_il, alog_il), (bcol, b_cm), (dcol, d_cm),
                    (gam_sb, gam_cm), (bet_sb, bet_cm), (r4_sb, r4_in),
                    (onesbf, onesbf_in), (onesf, onesf_in), (ident, ident_in),
                    (mA_sb, mA_in), (mB_sb, mB_in),
                ]:
                    S0.op("g", (lambda d=dst, s_=src: (lambda e: e.dma_start(
                        out=d[:, :], in_=s_)))(), inc="ld0", dma=True)
                ld0 = S0.v("ld0")
                S0.op("s", lambda e: e.activation(atmp[:, :], A_il[:, :], AF.Exp),
                      waits=[("ld0", ld0)], inc="qs")
                S0.op("v", lambda e: e.tensor_scalar(
                    A_il[:, :], atmp[:, :], -1.0, None, ALU.mult),
                    waits=[("qs", S0.v("qs"))], inc="qv")
                S0.op("v", lambda e: e.memset(states[:, :], 0.0), inc="qv")
                S0.op("v", lambda e: e.memset(epscol[:, :], cfg.eps), inc="qv")

                NCHT = nsplit(T)
                assert len(NCHT) <= 4
                sfree = {}
                for blk in range(BLKS):
                    p = blk % 2
                    sx = f"xa{p}"
                    w = [("ld0", ld0)] if blk < 2 else [("T", sfree[blk - 2])]
                    S0.op("y", (lambda p=p, b=blk: (lambda e: e.dma_start(
                        out=xst[p][:, :], in_=x_dt[b * 128:(b + 1) * 128, :])))(),
                        waits=w, inc=sx, dma=True)
                    S0.op("s", (lambda p=p: (lambda e: e.copy(
                        xbf[p][:, :], xst[p][:, :])))(),
                        waits=[(sx, S0.v(sx))], inc=None)
                    S0.op("s", (lambda p=p: (lambda e: e.square(
                        xsq[p][:, :], xbf[p][:, :])))(), inc="qs")
                    sw = [("qs", S0.v("qs"))]
                    for i, (o, n) in enumerate(NCHT):
                        S0.op("t", (lambda p=p, i=i, o=o, n=n, b=blk: (lambda e: e.matmul(
                            ps[i][0:1, 0:n], onesbf[:, :], xbf[p][:, o:o + n],
                            start=(b == 0), stop=(b == BLKS - 1))))(),
                            waits=sw if i == 0 else (), inc=None)
                        S0.op("t", (lambda p=p, i=i, o=o, n=n, b=blk: (lambda e: e.matmul(
                            ps[4 + i][0:1, 0:n], onesbf[:, :], xsq[p][:, o:o + n],
                            start=(b == 0), stop=(b == BLKS - 1))))(),
                            inc="T" if i == len(NCHT) - 1 else None)
                    sfree[blk] = S0.v("T")
                tw = [("T", S0.v("T"))]
                for i, (o, n) in enumerate(NCHT):
                    S0.op("s", (lambda i=i, o=o, n=n: (lambda e: e.activation(
                        mrow[0:1, o:o + n], ps[i][0:1, 0:n], AF.Copy, scale=1.0 / D)))(),
                        waits=tw if i == 0 else (), inc=None)
                    S0.op("s", (lambda i=i, o=o, n=n: (lambda e: e.activation(
                        e2row[0:1, o:o + n], ps[4 + i][0:1, 0:n], AF.Copy, scale=1.0 / D)))(),
                        inc="qs" if i == len(NCHT) - 1 else None)
                S0.op("v", lambda e: e.tensor_tensor(
                    trow[:, :], mrow[:, :], mrow[:, :], ALU.mult),
                    waits=[("qs", S0.v("qs"))], inc=None)
                S0.op("v", lambda e: e.tensor_tensor(
                    trow[:, :], e2row[:, :], trow[:, :], ALU.subtract), inc="qv")
                S0.op("s", lambda e: e.activation(
                    rrow[:, :], trow[:, :], AF.Ln, bias=epscol[0:1, 0:1]),
                    waits=[("qv", S0.v("qv"))], inc=None)
                S0.op("s", lambda e: e.activation(
                    rrow[:, :], rrow[:, :], AF.Exp, scale=-0.5), inc="qs")
                S0.op("v", lambda e: e.tensor_tensor(
                    trow[:, :], mrow[:, :], rrow[:, :], ALU.mult),
                    waits=[("qs", S0.v("qs"))], inc="qv")
                bw = [("qv", S0.v("qv"))]
                for i, (o, n) in enumerate(NCHT):
                    S0.op("t", (lambda o=o, n=n: (lambda e: e.matmul(
                        ps[0][:, 0:n], onesf[:, :], rrow[0:1, o:o + n],
                        start=True, stop=True)))(), waits=bw, inc="T")
                    S0.op("v", (lambda o=o, n=n: (lambda e: e.tensor_copy(
                        rbc[:, o:o + n], ps[0][:, 0:n])))(),
                        waits=[("T", S0.v("T"))], inc="qv")
                    S0.op("t", (lambda o=o, n=n: (lambda e: e.matmul(
                        ps[1][:, 0:n], onesf[:, :], trow[0:1, o:o + n],
                        start=True, stop=True)))(),
                        waits=[("qv", S0.v("qv"))], inc="T")
                    S0.op("v", (lambda o=o, n=n: (lambda e: e.tensor_copy(
                        mrbc[:, o:o + n], ps[1][:, 0:n])))(),
                        waits=[("T", S0.v("T"))], inc="qv")
                    bw = [("qv", S0.v("qv"))]
                S0.barrier()
                S0.emit()

            S1 = Sched(nc, sems, cnt)
            with ExitStack() as p1:
                p1sb = lambda n, sh, dt=F32: p1.enter_context(
                    nc.sbuf_tensor(n, list(sh), dt))
                xs1 = [p1sb(f"xs1_{i}", (128, TC)) for i in range(2)]
                xs2 = [p1sb(f"xs2_{i}", (128, TC)) for i in range(2)]
                tmpa = p1sb("tmpa", (128, TC))
                tmpb = tmpa
                xlnbf = [p1sb(f"xlnbf{i}", (128, TC), BF16) for i in range(2)]
                dtin = p1sb("dtin", (R, TC), BF16)
                xpB = p1sb("xpB", (32, TC), BF16)
                es = p1sb("es", (128, TC))
                dtb = [p1sb(f"dtb{i}", (128, TC)) for i in range(2)]
                xln2 = [p1sb(f"xln2_{i}", (128, TC)) for i in range(2)]
                ub = [p1sb(f"ub{i}", (128, TC), BF16) for i in range(2)]
                dtrep = [p1sb(f"dtrep{i}", (128, TC)) for i in range(2)]
                urep = [p1sb(f"urep{i}", (128, TC), BF16) for i in range(2)]
                BrepB = [p1sb(f"Brep{i}", (128, TC), BF16) for i in range(4)]
                CrepB = [p1sb(f"Crep{i}", (128, TC), BF16) for i in range(4)]
                dAb = [p1sb(f"dA{i}", (128, TC)) for i in range(4)]
                dBxb = [p1sb(f"dBx{i}", (128, TC), BF16) for i in range(4)]
                h_sb = p1sb("h_sb", (128, TC), BF16)
                hCb = [p1sb(f"hC{i}", (128, TC), BF16) for i in range(4)]
                yb0 = p1sb("yb0", (128, TC))
                yb = [yb0, yb0]

                for wt_d, wt_s, krows in [(upb, up_w, D), (downb, down_w, INT),
                                          (fc1b, fc1_w, D), (fc2b, fc2_w, LAT),
                                          (expb, expert_w, LAT)]:
                    for (ro, rn) in nsplit(krows, 1024):
                        S1.op("g", (lambda d=wt_d, s_=wt_s, ro=ro, rn=rn: (lambda e: e.dma_start(
                            out=d[ro:ro + rn, :], in_=s_[ro:ro + rn, :])))(),
                            inc="ld0", dma=True)
                precast_done = S1.v("ld0")
                NCHc = nsplit(TC)
                assert len(NCHc) <= 2
                xa_free = {}
                yps_free = {}
                ybl_free = {}
                xln_free = {}
                xp_mm = {}
                dtps_free = {}
                rep_free = {}
                dtb_cons = {}
                cons = {}
                xp_evac = 0
                bc_done = 0

                for c in range(NP):
                    co = c * TC
                    for blk in range(BLKS):
                        p = blk % 2
                        sx = f"xa{p}"
                        if (c, blk - 2) in xa_free:
                            w = [("qs", xa_free[(c, blk - 2)])]
                        elif c > 0:
                            w = [("qs", xa_free[(c - 1, BLKS - 2 + blk)])]
                        else:
                            w = []
                        S1.op("y", (lambda p=p, b=blk, co=co: (lambda e: e.dma_start(
                            out=xs1[p][:, :],
                            in_=x_dt[b * 128:(b + 1) * 128, co:co + TC])))(),
                            waits=w, inc=sx, dma=True)
                        S1.op("v", (lambda p=p, co=co: (lambda e: e.tensor_tensor(
                            tmpa[:, :], xs1[p][:, :], rbc[:, co:co + TC], ALU.mult)))(),
                            waits=[(sx, S1.v(sx))], inc=None)
                        S1.op("v", (lambda p=p, co=co: (lambda e: e.tensor_tensor(
                            xs1[p][:, :], tmpa[:, :], mrbc[:, co:co + TC], ALU.subtract)))(),
                            inc="qv")
                        cw = [("qv", S1.v("qv"))]
                        if (c, blk - 2) in xp_mm:
                            cw.append(("T", xp_mm[(c, blk - 2)]))
                        elif c > 0:
                            cw.append(("T", xp_mm[(c - 1, BLKS - 2 + blk)]))
                        S1.op("s", (lambda p=p: (lambda e: e.copy(
                            xlnbf[p][:, :], xs1[p][:, :])))(),
                            waits=cw, inc="qs")
                        xa_free[(c, blk)] = S1.v("qs")
                        sw = [("qs", S1.v("qs"))]
                        if c > 0 and blk == 0:
                            sw += [("qs", prev_S), ("qg", prev_G), ("qv", prev_V)]
                        for i, (o, n) in enumerate(NCHc):
                            S1.op("t", (lambda p=p, b=blk, o=o, n=n, i=i: (lambda e: e.matmul(
                                ps[i][:, 0:n], xpw_sb[:, b * RW:b * RW + 128],
                                xlnbf[p][:, o:o + n],
                                start=(b == 0), stop=(b == BLKS - 1))))(),
                                waits=sw if i == 0 else (), inc=None)
                            S1.op("t", (lambda p=p, b=blk, o=o, n=n, i=i: (lambda e: e.matmul(
                                ps[2 + i][0:32, 0:n],
                                xpw_sb[:, b * RW + 128:(b + 1) * RW],
                                xlnbf[p][:, o:o + n],
                                start=(b == 0), stop=(b == BLKS - 1))))(),
                                inc="T" if i == len(NCHc) - 1 else None)
                        xp_mm[(c, blk)] = S1.v("T")
                    tw = [("T", S1.v("T"))]
                    for i, (o, n) in enumerate(NCHc):
                        S1.op("v", (lambda o=o, n=n, i=i: (lambda e: e.tensor_copy(
                            dtin[:, o:o + n], ps[i][0:R, 0:n])))(),
                            waits=tw if i == 0 else (), inc=None)
                        S1.op("v", (lambda o=o, n=n, i=i: (lambda e: e.tensor_copy(
                            xpB[:, o:o + n], ps[2 + i][0:32, 0:n])))(),
                            inc="qv" if i == len(NCHc) - 1 else None)
                    xp_evac = S1.v("qv")
                    w = [("qv", xp_evac)]
                    if c > 0:
                        w.append(("bc", bc_done))
                    S1.op("y", lambda e: e.dma_start(out=xpB_h, in_=xpB[:, :]),
                          waits=w, inc="xst", dma=True)
                    xpB_st = S1.v("xst")
                    for q in range(4):
                        S1.op("s", (lambda q=q: (lambda e: e.dma_start(
                            out=BrepB[q][:, :],
                            in_=xpB_h[4 * q:4 * q + 4, :].unsqueeze(0).broadcast_to([32, 4, TC]))))(),
                            waits=[("xst", xpB_st)], inc="bc", dma=True)
                        S1.op("s", (lambda q=q: (lambda e: e.dma_start(
                            out=CrepB[q][:, :],
                            in_=xpB_h[16 + 4 * q:16 + 4 * q + 4, :].unsqueeze(0).broadcast_to([32, 4, TC]))))(),
                            inc="bc", dma=True)
                    bc_done = S1.v("bc")

                    psdt = [(ps[0], ps[1]), (ps[2], ps[3])]
                    psy = [(ps[4], ps[5]), (ps[6], ps[7])]
                    for blk in range(BLKS):
                        par = blk % 2
                        w = [("qv", xp_evac)]
                        if (c, blk - 2) in dtps_free:
                            w.append(("qs", dtps_free[(c, blk - 2)]))
                        elif c > 0:
                            w.append(("qs", dtps_free[(c - 1, BLKS - 2 + blk)]))
                        for i, (o, n) in enumerate(NCHc):
                            S1.op("t", (lambda b=blk, o=o, n=n, i=i, pr=par: (lambda e: e.matmul(
                                psdt[pr][i][:, 0:n], dtw_sb[:, b * 128:(b + 1) * 128],
                                dtin[:, o:o + n], start=True, stop=True)))(),
                                waits=w if i == 0 else (),
                                inc="T" if i == len(NCHc) - 1 else None)
                        sw = [("T", S1.v("T"))]
                        if (c, blk - 2) in dtb_cons:
                            r0, r1 = dtb_cons[(c, blk - 2)]
                            sw += [("rp0", r0), ("rp1", r1)]
                        elif c > 0:
                            r0, r1 = dtb_cons[(c - 1, BLKS - 2 + blk)]
                            sw += [("rp0", r0), ("rp1", r1)]
                        for i, (o, n) in enumerate(NCHc):
                            S1.op("s", (lambda b=blk, o=o, n=n, i=i, pr=par: (lambda e: e.activation(
                                es[:, o:o + n], psdt[pr][i][:, 0:n], AF.Exp,
                                bias=bcol[:, b:b + 1])))(),
                                waits=sw if i == 0 else (), inc=None)
                        S1.op("s", (lambda pr=par: (lambda e: e.activation(
                            dtb[pr][:, :], es[:, :], AF.Ln, bias=1.0)))(), inc="qs")
                        dtps_free[(c, blk)] = S1.v("qs")
                        dt_done = S1.v("qs")
                        sxb = f"xb{par}"
                        if (c, blk - 2) in xln_free:
                            w = [("qv", xln_free[(c, blk - 2)])]
                        elif c > 0:
                            w = [("qv", xln_free[(c - 1, BLKS - 2 + blk)])]
                        else:
                            w = []
                        S1.op("y", (lambda pr=par, b=blk, co=co: (lambda e: e.dma_start(
                            out=xs2[pr][:, :],
                            in_=x_dt[b * 128:(b + 1) * 128, co:co + TC])))(),
                            waits=w, inc=sxb, dma=True)
                        S1.op("v", (lambda pr=par, co=co: (lambda e: e.tensor_tensor(
                            tmpb[:, :], xs2[pr][:, :], rbc[:, co:co + TC], ALU.mult)))(),
                            waits=[(sxb, S1.v(sxb))], inc=None)
                        S1.op("v", (lambda pr=par, co=co: (lambda e: e.tensor_tensor(
                            xln2[pr][:, :], tmpb[:, :], mrbc[:, co:co + TC], ALU.subtract)))(),
                            inc=None)
                        S1.op("v", (lambda pr=par: (lambda e: e.tensor_tensor(
                            ub[pr][:, :], dtb[pr][:, :], xln2[pr][:, :], ALU.mult)))(),
                            waits=[("qs", dt_done)], inc="qv")
                        u_done = S1.v("qv")
                        for gl in range(4):
                            gp = gl % 2
                            rp = f"rp{gp}"
                            gslot = c * BLKS * 4 + blk * 4 + gl
                            w = [("qv", u_done), ("bc", bc_done)]
                            if gslot - 2 in rep_free:
                                rs, rg = rep_free[gslot - 2]
                                w += [("qs", rs), ("qv", rg)]
                            S1.op("s", (lambda pr=par, g=gp, gl=gl: (lambda e: e.dma_start(
                                out=dtrep[g][:, :],
                                in_=dtb[pr].ap()[32 * gl:32 * gl + 32, :].unsqueeze(1).broadcast_to([32, 4, TC]))))(),
                                waits=w, inc=rp, dma=True)
                            S1.op("s", (lambda pr=par, g=gp, gl=gl: (lambda e: e.dma_start(
                                out=urep[g][:, :],
                                in_=ub[pr].ap()[32 * gl:32 * gl + 32, :].unsqueeze(1).broadcast_to([32, 4, TC]))))(),
                                inc=rp, dma=True)
                            rep_rdy = S1.v(rp)
                            dtb_cons[(c, blk)] = (S1.v("rp0"), S1.v("rp1"))
                            lastS = lastG = 0
                            for q in range(4):
                                tile = (blk * 4 + gl) * 4 + q
                                gtile = c * G4 + tile
                                bp = gtile % 4
                                w = [(rp, rep_rdy)]
                                if ("dA", gtile - 4) in cons:
                                    w.append(("qv", cons[("dA", gtile - 4)]))
                                S1.op("s", (lambda g=gp, b=bp, col=tile: (lambda e: e.activation(
                                    dAb[b][:, :], dtrep[g][:, :], AF.Exp,
                                    scale=A_il[:, col:col + 1])))(),
                                    waits=w, inc="qs")
                                lastS = S1.v("qs")
                                w = [(rp, rep_rdy), ("bc", bc_done)]
                                if ("dBx", gtile - 4) in cons:
                                    w.append(("qv", cons[("dBx", gtile - 4)]))
                                S1.op("v", (lambda g=gp, b=bp, q=q: (lambda e: e.tensor_tensor(
                                    dBxb[b][:, :], urep[g][:, :], BrepB[q][:, :], ALU.mult)))(),
                                    waits=w, inc="qv")
                                lastG = S1.v("qv")
                                S1.op("v", (lambda b=bp, t=tile: (lambda e: e.tensor_tensor_scan(
                                    h_sb[:, :], dAb[b][:, :], dBxb[b][:, :],
                                    states[:, t:t + 1], ALU.mult, ALU.add)))(),
                                    waits=[("qs", lastS)], inc="qv")
                                cons[("dA", gtile)] = S1.v("qv")
                                cons[("dBx", gtile)] = S1.v("qv")
                                if c < NP - 1:
                                    S1.op("v", (lambda t=tile: (lambda e: e.tensor_copy(
                                        states[:, t:t + 1], h_sb[:, TC - 1:TC])))(),
                                        inc=None)
                                w = []
                                if ("hC", gtile - 4) in cons:
                                    w.append(("T", cons[("hC", gtile - 4)]))
                                S1.op("v", (lambda b=bp, q=q: (lambda e: e.tensor_tensor(
                                    hCb[b][:, :], h_sb[:, :], CrepB[q][:, :], ALU.mult)))(),
                                    waits=w, inc="qv")
                                vhc = S1.v("qv")
                                w = [("qv", vhc)]
                                if q == 0 and gl == 0:
                                    if (c, blk - 2) in yps_free:
                                        w.append(("qv", yps_free[(c, blk - 2)]))
                                    elif c > 0:
                                        w.append(("qv", yps_free[(c - 1, BLKS - 2 + blk)]))
                                for i, (o, n) in enumerate(NCHc):
                                    S1.op("t", (lambda b=bp, gl=gl, q=q, o=o, n=n, i=i, pr=par: (lambda e: e.matmul(
                                        psy[pr][i][32 * gl:32 * gl + 32, 0:n], r4_sb[:, :],
                                        hCb[b][:, o:o + n],
                                        start=(q == 0),
                                        stop=(q == 3),
                                        tile_position=(0, 32 * gl))))(),
                                        waits=w if i == 0 else (),
                                        inc="T" if i == len(NCHc) - 1 else None)
                                cons[("hC", gtile)] = S1.v("T")
                            rep_free[gslot] = (lastS, lastG)
                        syh = f"yh{par}"
                        pyh = f"yh{(blk - 1) % 2}"
                        w = [("T", S1.v("T"))]
                        if (c, blk - 1) in ybl_free:
                            w.append((pyh, ybl_free[(c, blk - 1)]))
                        elif c > 0:
                            w.append((pyh, ybl_free[(c - 1, BLKS - 1)]))
                        for i, (o, n) in enumerate(NCHc):
                            S1.op("v", (lambda pr=par, b=blk, o=o, n=n, i=i: (lambda e: e.scalar_tensor_tensor(
                                yb[pr][:, o:o + n], xln2[pr][:, o:o + n],
                                dcol[:, b:b + 1], psy[pr][i][:, 0:n],
                                ALU.mult, ALU.add)))(),
                                waits=w if i == 0 else (),
                                inc="qv" if i == len(NCHc) - 1 else None)
                        xln_free[(c, blk)] = S1.v("qv")
                        yps_free[(c, blk)] = S1.v("qv")
                        S1.op("y", (lambda pr=par, b=blk, co=co: (lambda e: e.dma_start(
                            out=y_h[b * 128:(b + 1) * 128, co:co + TC],
                            in_=yb[pr][:, :])))(),
                            waits=[("qv", S1.v("qv"))], inc=syh, dma=True)
                        ybl_free[(c, blk)] = S1.v(syh)
                        if c == 0 and blk == 0 and DEBUG_DUMP:
                            dw = [("qv", S1.v("qv")), ("qs", S1.v("qs")),
                                  ("qg", S1.v("qg")), ("T", S1.v("T"))]
                            for di, src_t in enumerate([dAb[1], dBxb[1], h_sb,
                                                        hCb[1], BrepB[3], CrepB[3],
                                                        dtrep[1], urep[1]]):
                                S1.op("g", (lambda di=di, st=src_t: (lambda e: e.dma_start(
                                    out=dbg_h[:, di * TC:(di + 1) * TC], in_=st[:, :])))(),
                                    waits=dw if di == 0 else (), inc="ld1", dma=True)
                            S1.barrier()
                    prev_S, prev_G, prev_V = S1.v("qs"), S1.v("qg"), S1.v("qv")
                S1.barrier()
                S1.emit()

        S2 = Sched(nc, sems, cnt)
        with ExitStack() as p2:
            p2sb = lambda n, sh, dt=F32: p2.enter_context(nc.sbuf_tensor(n, list(sh), dt))
            arenaA = p2sb("arenaA", (128, BLKS * Tf))
            xlnff = arenaA.ap().bitcast(BF16)
            s_sb = arenaA
            lat_sb = arenaA.ap().bitcast(BF16)[:, BLKS * Tf:BLKS * Tf + (LAT // 128) * Tf]
            moe_sb = p2sb("moe_sb", (128, (LAT // 128) * Tf), BF16)
            NKW = KROW // 128
            wb = [p2sb(f"wb{i}", (128, NKW * 512), BF16) for i in range(2)]
            hio = [p2sb(f"hio{i}", (128, NKW * Tf), BF16) for i in range(2)]
            xa = [p2sb(f"xa{i}", (128, Tf)) for i in range(3)]
            yba = [p2sb(f"yba{i}", (128, Tf)) for i in range(3)]
            ybb = [p2sb(f"ybb{i}", (128, Tf)) for i in range(3)]
            x1b = [p2sb(f"x1b{i}", (128, Tf)) for i in range(3)]
            ob = [p2sb(f"ob{i}", (128, Tf)) for i in range(2)]
            ft1 = p2sb("ft1", (128, Tf))
            ft2 = p2sb("ft2", (128, Tf))
            ft3 = ft1
            fu = [p2sb(f"fu{i}", (128, Tf)) for i in range(2)]
            bca = p2sb("bca", (128, Tf))
            bcb = p2sb("bcb", (128, Tf))
            sbf = [p2sb(f"sbf{i}", (128, Tf), BF16) for i in range(3)]
            ssq = [p2sb(f"ssq{i}", (128, Tf), BF16) for i in range(3)]
            MCH = nsplit(Tf, 128)
            rwcols = p2sb("rwcols", (128, len(MCH)))
            mx8 = p2sb("mx8", (128, 8))
            rwr1 = None
            rwbc = p2sb("rwbc", (128, Tf), BF16)
            m2r = p2sb("m2r", (1, Tf))
            e2r = p2sb("e2r", (1, Tf))
            r2r = e2r
            t2r = p2sb("t2r", (1, Tf))
            rwr1 = t2r
            assert Tf <= 512

            wbi = [0]
            ftrot = [0]
            fu_free = {}
            wb_free = {}
            hio_free = {}
            ws_ps_free = {}
            x1_done = {}
            x1c = {}
            odone = {}
            stat_done = {}
            NMG_UP = len(nsplit(INT, 512))

            def rows_and_bc(S, waits):
                S.op("s", lambda e: e.activation(
                    m2r[0:1, 0:Tf], ps[0][0:1, 0:Tf], AF.Copy, scale=1.0 / D),
                    waits=waits, inc=None)
                S.op("s", lambda e: e.activation(
                    e2r[0:1, 0:Tf], ps[1][0:1, 0:Tf], AF.Copy, scale=1.0 / D),
                    inc="qs")
                S.op("v", lambda e: e.tensor_tensor(
                    t2r[:, :], m2r[:, :], m2r[:, :], ALU.mult),
                    waits=[("qs", S.v("qs"))], inc=None)
                S.op("v", lambda e: e.tensor_tensor(
                    t2r[:, :], e2r[:, :], t2r[:, :], ALU.subtract), inc="qv")
                S.op("s", lambda e: e.activation(
                    r2r[:, :], t2r[:, :], AF.Ln, bias=epscol[0:1, 0:1]),
                    waits=[("qv", S.v("qv"))], inc=None)
                S.op("s", lambda e: e.activation(
                    r2r[:, :], r2r[:, :], AF.Exp, scale=-0.5), inc="qs")
                S.op("v", lambda e: e.tensor_tensor(
                    t2r[:, :], m2r[:, :], r2r[:, :], ALU.mult),
                    waits=[("qs", S.v("qs"))], inc="qv")
                S.op("t", lambda e: e.matmul(
                    ps[2][:, 0:Tf], onesf[:, :], r2r[0:1, 0:Tf], start=True, stop=True),
                    waits=[("qv", S.v("qv"))], inc="T")
                S.op("v", lambda e: e.tensor_copy(bca[:, :], ps[2][:, 0:Tf]),
                     waits=[("T", S.v("T"))], inc="qv")
                S.op("t", lambda e: e.matmul(
                    ps[3][:, 0:Tf], onesf[:, :], t2r[0:1, 0:Tf], start=True, stop=True),
                    waits=[("qv", S.v("qv"))], inc="T")
                S.op("v", lambda e: e.tensor_copy(bcb[:, :], ps[3][:, 0:Tf]),
                     waits=[("T", S.v("T"))], inc="qv")
                return S.v("qv")

            def ws(S, wd, Krows, OD, rhs_of_k, evac, extra_w=()):
                for mg, (mgo, mgn) in enumerate(nsplit(OD, 512)):
                    nml = (mgn + 127) // 128
                    rch = nsplit(Krows, KROW)
                    base = (mg % 2) * 4
                    for ri, (ro, rn) in enumerate(rch):
                        wi = wbi[0] % 2
                        wbi[0] += 1
                        sld = f"wl{wi}"
                        w = list(extra_w) + [("ld0", precast_done)]
                        if wi in wb_free:
                            w.append(("T", wb_free[wi]))
                        S.op("s", (lambda wd=wd, ro=ro, rn=rn, mgo=mgo, mgn=mgn, wi=wi: (lambda e: e.dma_start(
                            out=wb[wi][:, 0:(rn // 128) * mgn],
                            in_=wd[ro:ro + rn, mgo:mgo + mgn].rearrange("(k p) c -> p k c", p=128))))(),
                            waits=w, inc=sld, dma=True)
                        lw = [(sld, S.v(sld))]
                        if ri == 0 and mg >= 2:
                            lw.append(("qs", ws_ps_free.get((mg - 2, "S"), 0)))
                            lw.append(("qv", ws_ps_free.get((mg - 2, "V"), 0)))
                        for k in range(rn // 128):
                            for ml in range(nml):
                                mw = min(128, mgn - ml * 128)
                                S.op("t", (lambda wi=wi, k=k, ml=ml, mw=mw, mgn=mgn, ro=ro,
                                           pt=ps[base + ml], rhs=rhs_of_k,
                                           first=(ri == 0 and k == 0),
                                           last=(ri == len(rch) - 1 and k == rn // 128 - 1): (lambda e: e.matmul(
                                    pt[0:mw, 0:Tf],
                                    wb[wi][:, k * mgn + ml * 128:k * mgn + ml * 128 + mw],
                                    rhs(ro // 128 + k), start=first, stop=last)))(),
                                    waits=lw if (k == 0 and ml == 0) else (),
                                    inc="T" if (k == rn // 128 - 1 and ml == nml - 1) else None)
                        wb_free[wi] = S.v("T")
                    evac(mg, mgo, mgn, nml, base, [("T", S.v("T"))])

            for pss in range(NP):
                pc = pss * Tf
                yA = pss * TC
                yB = pss * TC + Tf

                for blk in range(BLKS):
                    par = blk % 3
                    sxy, sxs = f"xy{par}", f"xs{par}"
                    w = [(sxs, x1_done[par])] if par in x1_done else []
                    S2.op("y", (lambda p=par, b=blk, pc=pc: (lambda e: e.dma_start(
                        out=xa[p][:, :], in_=xcols[b * 128:(b + 1) * 128, pc:pc + Tf])))(),
                        waits=w, inc=sxy, dma=True)
                    S2.op("y", (lambda p=par, b=blk, yA=yA: (lambda e: e.dma_start(
                        out=yba[p][:, :], in_=y_h[b * 128:(b + 1) * 128, yA:yA + Tf])))(),
                        inc=sxy, dma=True)
                    S2.op("y", (lambda p=par, b=blk, yB=yB: (lambda e: e.dma_start(
                        out=ybb[p][:, :], in_=y_h[b * 128:(b + 1) * 128, yB:yB + Tf])))(),
                        inc=sxy, dma=True)
                    stw = [(sxy, S2.v(sxy))]
                    if par in x1c:
                        stw.append(("qs", x1c[par]))
                    S2.op("v", (lambda p=par: (lambda e: e.scalar_tensor_tensor(
                        ft3[:, :], yba[p][:, :], mA_sb[:, 0:1], xa[p][:, :],
                        ALU.mult, ALU.add)))(),
                        waits=stw, inc=None)
                    S2.op("v", (lambda p=par: (lambda e: e.scalar_tensor_tensor(
                        x1b[p][:, :], ybb[p][:, :], mB_sb[:, 0:1], ft3[:, :],
                        ALU.mult, ALU.add)))(), inc="qv")
                    S2.op("y", (lambda p=par, b=blk, pc=pc: (lambda e: e.dma_start(
                        out=x1_h[b * 128:(b + 1) * 128, pc:pc + Tf], in_=x1b[p][:, :])))(),
                        waits=[("qv", S2.v("qv"))], inc=sxs, dma=True)
                    x1_done[par] = S2.v(sxs)
                    w = [("qv", S2.v("qv"))]
                    if par in stat_done:
                        w.append(("T", stat_done[par]))
                    S2.op("s", (lambda p=par: (lambda e: e.copy(
                        sbf[p][:, :], x1b[p][:, :])))(), waits=w, inc=None)
                    S2.op("s", (lambda p=par: (lambda e: e.square(
                        ssq[p][:, :], sbf[p][:, :])))(), inc="qs")
                    x1c[par] = S2.v("qs")
                    sw = [("qs", S2.v("qs"))]
                    S2.op("t", (lambda p=par, b=blk: (lambda e: e.matmul(
                        ps[0][0:1, 0:Tf], onesbf[:, :], sbf[p][:, :],
                        start=(b == 0), stop=(b == BLKS - 1))))(),
                        waits=sw, inc=None)
                    S2.op("t", (lambda p=par, b=blk: (lambda e: e.matmul(
                        ps[1][0:1, 0:Tf], onesbf[:, :], ssq[p][:, :],
                        start=(b == 0), stop=(b == BLKS - 1))))(), inc="T")
                    stat_done[par] = S2.v("T")
                stats_qs = S2.v("qs")
                rows_and_bc(S2, [("T", S2.v("T"))])
                xe = {}
                for blk in range(BLKS):
                    par = blk % 2
                    sxr = f"xr{par}"
                    w = ([("qv", S2.v("qv")), ("qs", stats_qs)] +
                         [(f"xs{j}", x1_done[j]) for j in x1_done]
                         if blk < 2 else [("qv", xe[blk - 2])])
                    S2.op("y", (lambda p=par, b=blk, pc=pc: (lambda e: e.dma_start(
                        out=x1b[p][:, :], in_=x1_h[b * 128:(b + 1) * 128, pc:pc + Tf])))(),
                        waits=w, inc=sxr, dma=True)
                    S2.op("v", (lambda p=par: (lambda e: e.tensor_tensor(
                        ft1[:, :], x1b[p][:, :], bca[:, :], ALU.mult)))(),
                        waits=[(sxr, S2.v(sxr))], inc=None)
                    S2.op("v", (lambda b=blk: (lambda e: e.tensor_tensor(
                        xlnff[:, b * Tf:(b + 1) * Tf], ft1[:, :], bcb[:, :], ALU.subtract)))(),
                        inc="qv")
                    xe[blk] = S2.v("qv")
                xln_done = S2.v("qv")

                gdone = {}
                for mi, (mo, mn) in enumerate(MCH):
                    pg = 4 + (mi % 2)
                    w = [("qv", xln_done)]
                    if mi - 2 in gdone:
                        w.append(("qv", gdone[mi - 2]))
                    for k in range(BLKS):
                        S2.op("t", (lambda k=k, mo=mo, mn=mn, pg=pg: (lambda e: e.matmul(
                            ps[pg][0:mn, 0:NE], xlnff[:, k * Tf + mo:k * Tf + mo + mn],
                            gate_sb[:, k * NE:(k + 1) * NE],
                            start=(k == 0), stop=(k == BLKS - 1))))(),
                            waits=w if k == 0 else (),
                            inc="T" if k == BLKS - 1 else None)
                    S2.op("v", (lambda pg=pg, mn=mn: (lambda e: e.max(
                        mx8[0:mn, :], ps[pg][0:mn, 0:NE])))(),
                        waits=[("T", S2.v("T"))], inc=None)
                    S2.op("v", (lambda mi=mi, mn=mn: (lambda e: e.tensor_tensor(
                        rwcols[0:mn, mi:mi + 1], mx8[0:mn, 0:1], mx8[0:mn, 1:2], ALU.add)))(),
                        inc="qv")
                    gdone[mi] = S2.v("qv")
                for mi, (mo, mn) in enumerate(MCH):
                    S2.op("t", (lambda mi=mi, mn=mn: (lambda e: e.transpose(
                        ps[6][0:1, 0:mn], rwcols[0:mn, mi:mi + 1], ident[0:mn, 0:mn])))(),
                        waits=[("qv", S2.v("qv"))], inc="T")
                    S2.op("v", (lambda mn=mn: (lambda e: e.tensor_copy(
                        rwr1[0:1, 0:mn], ps[6][0:1, 0:mn])))(),
                        waits=[("T", S2.v("T"))], inc="qv")
                    S2.op("t", (lambda mi=mi, mo=mo, mn=mn: (lambda e: e.matmul(
                        ps[7][:, mo:mo + mn], onesf[:, :], rwr1[0:1, 0:mn],
                        start=(mi == 0), stop=(mi == len(MCH) - 1))))(),
                        waits=[("qv", S2.v("qv"))], inc="T")
                S2.op("v", lambda e: e.tensor_copy(rwbc[:, :], ps[7][:, 0:Tf]),
                      waits=[("T", S2.v("T"))], inc="qv")
                rw_done = S2.v("qv")

                def evac_fc1(mg, mgo, mgn, nml, base, tw):
                    for ml in range(nml):
                        mw = min(128, mgn - ml * 128)
                        S2.op("s", (lambda mg=mg, ml=ml, mw=mw, pt=ps[base + ml]: (lambda e: e.copy(
                            lat_sb[0:mw, (mg * 4 + ml) * Tf:(mg * 4 + ml + 1) * Tf],
                            pt[0:mw, 0:Tf])))(),
                            waits=tw if ml == 0 else (),
                            inc="qs" if ml == nml - 1 else None)
                    ws_ps_free[(mg, "S")] = S2.v("qs")
                    ws_ps_free[(mg, "V")] = S2.v("qv")

                ws(S2, fc1b, D, LAT, lambda k: xlnff[:, k * Tf:(k + 1) * Tf],
                   evac_fc1, extra_w=[("qv", xln_done), ("qs", S2.v("qs"))])
                lat_done = S2.v("qs")

                def evac_exp(mg, mgo, mgn, nml, base, tw):
                    for ml in range(nml):
                        mw = min(128, mgn - ml * 128)
                        S2.op("v", (lambda mg=mg, ml=ml, mw=mw, pt=ps[base + ml]: (lambda e: e.tensor_tensor(
                            moe_sb[0:mw, (mg * 4 + ml) * Tf:(mg * 4 + ml + 1) * Tf],
                            pt[0:mw, 0:Tf], rwbc[0:mw, :], ALU.mult)))(),
                            waits=tw if ml == 0 else (),
                            inc="qv" if ml == nml - 1 else None)
                    ws_ps_free[(mg, "S")] = S2.v("qs")
                    ws_ps_free[(mg, "V")] = S2.v("qv")

                ws(S2, expb, LAT, LAT, lambda k: lat_sb[:, k * Tf:(k + 1) * Tf],
                   evac_exp, extra_w=[("qs", lat_done), ("qv", rw_done)])
                moe_done = S2.v("qv")

                def evac_up(mg, mgo, mgn, nml, base, tw):
                    hb = mg % 2
                    shs = f"hs{hb}"
                    w = list(tw)
                    if (pss, mg - 2) in hio_free:
                        w.append((shs, hio_free[(pss, mg - 2)]))
                    elif pss > 0:
                        w.append((shs, hio_free[(pss - 1, NMG_UP - 2 + mg)]))
                    for ml in range(nml):
                        fi = ftrot[0] % 2
                        ftrot[0] += 1
                        fa = fu[fi]
                        sw = list(w) if ml == 0 else []
                        if fi in fu_free:
                            sw.append(("qv", fu_free[fi]))
                        S2.op("s", (lambda fa=fa, pt=ps[base + ml]: (lambda e: e.activation(
                            fa[:, :], pt[:, 0:Tf], AF.Relu)))(),
                            waits=sw, inc="qs")
                        S2.op("v", (lambda fa=fa, ml=ml, hb=hb, pt=ps[base + ml]: (lambda e: e.tensor_tensor(
                            hio[hb][:, ml * Tf:(ml + 1) * Tf], fa[:, :], pt[0:128, 0:Tf],
                            ALU.mult)))(),
                            waits=[("qs", S2.v("qs"))], inc="qv")
                        fu_free[fi] = S2.v("qv")
                    ws_ps_free[(mg, "S")] = S2.v("qs")
                    ws_ps_free[(mg, "V")] = S2.v("qv")
                    S2.op("y", (lambda mg=mg, hb=hb, nml=nml: (lambda e: e.dma_start(
                        out=h_h[mg * 512:mg * 512 + nml * 128, :].rearrange(
                            "(m p) t -> p m t", p=128),
                        in_=hio[hb][:, 0:nml * Tf])))(),
                        waits=[("qv", S2.v("qv"))], inc=shs, dma=True)
                    hio_free[(pss, mg)] = S2.v(shs)

                ws(S2, upb, D, INT, lambda k: xlnff[:, k * Tf:(k + 1) * Tf],
                   evac_up, extra_w=[("qv", moe_done)])
                up_done = [("hs0", S2.v("hs0")), ("hs1", S2.v("hs1"))]

                sdone = {}
                hl_free = {}
                for mg, (mgo, mgn) in enumerate(nsplit(D, 512)):
                    nml = (mgn + 127) // 128
                    base = (mg % 2) * 4
                    rch = nsplit(INT, KROW)
                    for ri, (ro, rn) in enumerate(rch):
                        wi = wbi[0] % 2
                        wbi[0] += 1
                        sld = f"wl{wi}"
                        hb = ri % 2
                        shl = f"hl{hb}"
                        w = list(up_done)
                        if wi in wb_free:
                            w.append(("T", wb_free[wi]))
                        S2.op("s", (lambda ro=ro, rn=rn, mgo=mgo, mgn=mgn, wi=wi: (lambda e: e.dma_start(
                            out=wb[wi][:, 0:(rn // 128) * mgn],
                            in_=downb[ro:ro + rn, mgo:mgo + mgn].rearrange(
                                "(k p) c -> p k c", p=128))))(),
                            waits=w, inc=sld, dma=True)
                        hw = list(up_done)
                        key = (mg, ri - 2) if ri >= 2 else (mg - 1, len(rch) - 2 + ri)
                        if key in hl_free:
                            hw.append(("T", hl_free[key]))
                        S2.op("y", (lambda ro=ro, rn=rn, hb=hb: (lambda e: e.dma_start(
                            out=hio[hb][:, 0:(rn // 128) * Tf],
                            in_=h_h[ro:ro + rn, :].rearrange("(k p) t -> p k t", p=128))))(),
                            waits=hw, inc=shl, dma=True)
                        lw = [(sld, S2.v(sld)), (shl, S2.v(shl))]
                        if ri == 0 and mg >= 2:
                            lw.append(("qv", sdone[mg - 2]))
                        for k in range(rn // 128):
                            for ml in range(nml):
                                mw = min(128, mgn - ml * 128)
                                S2.op("t", (lambda wi=wi, hb=hb, k=k, ml=ml, mw=mw, mgn=mgn,
                                            pt=ps[base + ml],
                                            first=(ri == 0 and k == 0): (lambda e: e.matmul(
                                    pt[0:mw, 0:Tf],
                                    wb[wi][:, k * mgn + ml * 128:k * mgn + ml * 128 + mw],
                                    hio[hb][:, k * Tf:(k + 1) * Tf],
                                    start=first, stop=False)))(),
                                    waits=lw if (k == 0 and ml == 0) else (),
                                    inc="T" if (k == rn // 128 - 1 and ml == nml - 1) else None)
                        wb_free[wi] = S2.v("T")
                        hl_free[(mg, ri)] = S2.v("T")
                    rch2 = nsplit(LAT, KROW)
                    for ri2, (ro2, rn2) in enumerate(rch2):
                        wi = wbi[0] % 2
                        wbi[0] += 1
                        sld = f"wl{wi}"
                        w = [("qv", moe_done)]
                        if wi in wb_free:
                            w.append(("T", wb_free[wi]))
                        S2.op("s", (lambda ro2=ro2, rn2=rn2, mgo=mgo, mgn=mgn, wi=wi: (lambda e: e.dma_start(
                            out=wb[wi][:, 0:(rn2 // 128) * mgn],
                            in_=fc2b[ro2:ro2 + rn2, mgo:mgo + mgn].rearrange("(k p) c -> p k c", p=128))))(),
                            waits=w, inc=sld, dma=True)
                        lw = [(sld, S2.v(sld))]
                        for k in range(rn2 // 128):
                            for ml in range(nml):
                                mw = min(128, mgn - ml * 128)
                                S2.op("t", (lambda wi=wi, k=k, kk=ro2 // 128 + k, ml=ml, mw=mw, mgn=mgn,
                                            pt=ps[base + ml],
                                            last=(ri2 == len(rch2) - 1 and k == rn2 // 128 - 1): (lambda e: e.matmul(
                                    pt[0:mw, 0:Tf],
                                    wb[wi][:, k * mgn + ml * 128:k * mgn + ml * 128 + mw],
                                    moe_sb[:, kk * Tf:(kk + 1) * Tf],
                                    start=False, stop=last)))(),
                                    waits=lw if (k == 0 and ml == 0) else (),
                                    inc="T" if (k == rn2 // 128 - 1 and ml == nml - 1) else None)
                        wb_free[wi] = S2.v("T")
                    tw = [("T", S2.v("T"))]
                    for ml in range(nml):
                        mw = min(128, mgn - ml * 128)
                        S2.op("v", (lambda mg=mg, ml=ml, mw=mw, pt=ps[base + ml]: (lambda e: e.tensor_copy(
                            s_sb[0:mw, (mg * 4 + ml) * Tf:(mg * 4 + ml + 1) * Tf],
                            pt[0:mw, 0:Tf])))(),
                            waits=tw if ml == 0 else (),
                            inc="qv" if ml == nml - 1 else None)
                    sdone[mg] = S2.v("qv")
                s_done = S2.v("qv")

                st3 = {}
                for blk in range(BLKS):
                    par = blk % 2
                    w = [("qv", s_done)]
                    if blk - 2 in st3:
                        w.append(("T", st3[blk - 2]))
                    S2.op("s", (lambda p=par, b=blk: (lambda e: e.copy(
                        sbf[p][:, :], s_sb[:, b * Tf:(b + 1) * Tf])))(),
                        waits=w, inc=None)
                    S2.op("s", (lambda p=par: (lambda e: e.square(
                        ssq[p][:, :], sbf[p][:, :])))(), inc="qs")
                    sw = [("qs", S2.v("qs"))]
                    S2.op("t", (lambda p=par, b=blk: (lambda e: e.matmul(
                        ps[0][0:1, 0:Tf], onesbf[:, :], sbf[p][:, :],
                        start=(b == 0), stop=(b == BLKS - 1))))(),
                        waits=sw, inc=None)
                    S2.op("t", (lambda p=par, b=blk: (lambda e: e.matmul(
                        ps[1][0:1, 0:Tf], onesbf[:, :], ssq[p][:, :],
                        start=(b == 0), stop=(b == BLKS - 1))))(), inc="T")
                    st3[blk] = S2.v("T")
                rows_and_bc(S2, [("T", S2.v("T"))])
                for blk in range(BLKS):
                    par = blk % 2
                    sxr, sot = f"xr{par}", f"ot{par}"
                    w = [("qv", S2.v("qv"))]
                    if (pss, blk - 2) in odone:
                        w.append((sot, odone[(pss, blk - 2)]))
                    elif pss > 0:
                        w.append((sot, odone[(pss - 1, BLKS - 2 + par)]))
                    S2.op("y", (lambda p=par, b=blk, pc=pc: (lambda e: e.dma_start(
                        out=x1b[p][:, :], in_=x1_h[b * 128:(b + 1) * 128, pc:pc + Tf])))(),
                        waits=w, inc=sxr, dma=True)
                    vw = [(sxr, S2.v(sxr))]
                    if (pss, blk - 2) in odone:
                        vw.append((sot, odone[(pss, blk - 2)]))
                    elif pss > 0:
                        vw.append((sot, odone[(pss - 1, BLKS - 2 + par)]))
                    S2.op("v", (lambda b=blk: (lambda e: e.tensor_tensor(
                        ft1[:, :], s_sb[:, b * Tf:(b + 1) * Tf], bca[:, :], ALU.mult)))(),
                        waits=vw, inc=None)
                    S2.op("v", lambda e: e.tensor_tensor(
                        ft2[:, :], ft1[:, :], bcb[:, :], ALU.subtract), inc=None)
                    S2.op("v", (lambda b=blk: (lambda e: e.tensor_scalar(
                        ft1[:, :], ft2[:, :], gam_sb[:, b:b + 1], bet_sb[:, b:b + 1],
                        ALU.mult, ALU.add)))(), inc=None)
                    S2.op("v", (lambda p=par: (lambda e: e.tensor_tensor(
                        ob[p][:, :], ft1[:, :], x1b[p][:, :], ALU.add)))(), inc="qv")
                    S2.op("y", (lambda p=par, b=blk, pc=pc: (lambda e: e.dma_start(
                        out=out_dt[b * 128:(b + 1) * 128, pc:pc + Tf], in_=ob[p][:, :])))(),
                        waits=[("qv", S2.v("qv"))], inc=sot, dma=True)
                    odone[(pss, blk)] = S2.v(sot)
            S2.op("y", lambda e: e.nop(),
                  waits=[("ot0", S2.v("ot0")), ("ot1", S2.v("ot1"))])
            S2.emit()
    return nc



def _stage_core(cfg, inputs, b, half):
    D, T = cfg.D, cfg.T
    BLKS, TC, Tf, G4 = cfg.BLKS, cfg.TC, cfg.Tf, cfg.G4
    f32 = np.float32
    x = np.asarray(inputs["L0_attn_x"][b], f32)
    x_dt = np.ascontiguousarray(x.T)
    tok = np.concatenate([np.arange(p * TC + half * Tf, p * TC + half * Tf + Tf)
                          for p in range(cfg.NPASS)])
    xcols = np.ascontiguousarray(x_dt[:, tok])

    A_log = np.asarray(inputs["L0_attn_A_log"], f32)
    Ar = A_log.reshape(D // 32, 32, 4, 4)
    alog_il = np.ascontiguousarray(
        Ar.transpose(1, 3, 0, 2).reshape(128, G4))

    cm = lambda v: np.ascontiguousarray(np.asarray(v, f32).reshape(BLKS, 128).T)
    r4 = np.zeros((128, 32), f32)
    for p in range(128):
        r4[p, p // 4] = 1.0

    z = np.zeros((128, 1), f32)
    return {
        "x_dt": x_dt,
        "xcols": xcols,
        "up_w": np.ascontiguousarray(inputs["L0_ffn_up_w"], f32),
        "down_w": np.ascontiguousarray(inputs["L0_ffn_down_w"], f32),
        "fc1_w": np.ascontiguousarray(inputs["L0_ffn_fc1_w"], f32),
        "fc2_w": np.ascontiguousarray(inputs["L0_ffn_fc2_w"], f32),
        "expert_w": np.ascontiguousarray(inputs["L0_ffn_expert_w"], f32),
        "gate_w": np.ascontiguousarray(inputs["L0_ffn_gate_w"], f32),
        "xpw_t": np.ascontiguousarray(np.asarray(inputs["L0_attn_x_proj_w"], f32).T),
        "dtw_t": np.ascontiguousarray(np.asarray(inputs["L0_attn_dt_proj_w"], f32).T),
        "alog_il": alog_il,
        "b_cm": cm(inputs["L0_attn_dt_proj_b"]),
        "d_cm": cm(inputs["L0_attn_D"]),
        "gam_cm": cm(inputs["L0_ffn_ln_gamma"]),
        "bet_cm": cm(inputs["L0_ffn_ln_beta"]),
        "r4": r4.astype(ml_dtypes.bfloat16),
        "ones_bf": np.ones((128, 1), ml_dtypes.bfloat16),
        "ones_f": np.ones((1, 128), f32),
        "ident_f": np.eye(128, dtype=f32),
        "mask_a": z + (0.0 if half else 1.0),
        "mask_b": z + (1.0 if half else 0.0),
    }, tok


def kernel(**inputs):
    cfg = CFG
    nc = build_core_program(cfg)
    in_maps, toks = [], []
    for k in range(8):
        im, tok = _stage_core(cfg, inputs, k // 2, k % 2)
        in_maps.append(im)
        toks.append(tok)
    res = run_bass_kernel_spmd(nc, in_maps, core_ids=list(range(8)))
    B = inputs["L0_attn_x"].shape[0]
    out = np.empty((B, cfg.T, cfg.D), np.float32)
    for k in range(8):
        out[k // 2, toks[k], :] = np.asarray(res.results[k]["out"]).T
    return out
